# version 51
# baseline (speedup 1.0000x reference)
import numpy as np
from contextlib import ExitStack

import concourse.bass as bass
import concourse.mybir as mybir
from concourse.bass import IndirectOffsetOnAxis
from concourse.bass_utils import run_bass_kernel_spmd

F32 = mybir.dt.float32
BF16 = mybir.dt.bfloat16
I32 = mybir.dt.int32
ALU = mybir.AluOpType
AF = mybir.ActivationFunctionType

GELU_C2 = 0.044715
GELU_TANH_SCALE = 0.7978845608028654
LN_EPS = 1e-5
BIGV = 1.0e6

CFG_FULL = dict(D=1024, F=4096, SEQ=4096, ROWS=2, BI=23, LOHI=1.0)


def build_core_program(cfg):
    D, F, SEQ, ROWS = cfg["D"], cfg["F"], cfg["SEQ"], cfg["ROWS"]
    BI, LOHI = cfg["BI"], cfg["LOHI"]
    KC = D // 128
    FC = F // 128
    T = ROWS * SEQ
    NT = T // 128
    RC = SEQ // 128
    KTGT = SEQ // 2
    CT = ROWS * KTGT
    CNT = CT // 128
    GFC = 4
    NG = FC // GFC
    GW = GFC * 128
    halves = [(hh, nn, min(512, D - nn))
              for hh, nn in enumerate(range(0, D, 512))]

    nc = bass.Bass("TRN2", target_bir_lowering=False)

    x_d = nc.dram_tensor("x", [T, D], F32, kind="ExternalInput").ap()
    out_d = nc.dram_tensor("out", [T, D], F32, kind="ExternalOutput").ap()
    w1_d = nc.dram_tensor("w1", [D, D], F32, kind="ExternalInput").ap()
    b1_d = nc.dram_tensor("b1", [1, D], F32, kind="ExternalInput").ap()
    w23_d = nc.dram_tensor("w23", [D, 32], F32, kind="ExternalInput").ap()
    b23_d = nc.dram_tensor("b23", [1, 32], F32, kind="ExternalInput").ap()
    wr1_d = nc.dram_tensor("wr1", [1, D], F32, kind="ExternalInput").ap()
    w1f_d = nc.dram_tensor("w1f", [D, F], F32, kind="ExternalInput").ap()
    b1fp_d = nc.dram_tensor("b1fp", [128, FC], F32, kind="ExternalInput").ap()
    w2f_d = nc.dram_tensor("w2f", [F, D], F32, kind="ExternalInput").ap()
    b2f_d = nc.dram_tensor("b2f", [1, D], F32, kind="ExternalInput").ap()
    s_d = nc.dram_tensor("s_sc", [T, D], BF16, kind="Internal").ap()
    st3_d = nc.dram_tensor("st3", [T, 3], F32, kind="Internal").ap()
    imap_d = nc.dram_tensor("imap", [CT, 1], I32, kind="Internal").ap()
    if cfg.get("DBG"):
        dbg_d = nc.dram_tensor("dbg", [128, 6, NT], F32,
                               kind="ExternalOutput").ap()

    with ExitStack() as ctx:
        e = ctx.enter_context
        w1_b = e(nc.sbuf_tensor("w1_b", [128, KC, D], BF16))
        w23_b = e(nc.sbuf_tensor("w23_b", [128, KC, 32], BF16))
        w1f_b = e(nc.sbuf_tensor("w1f_b", [128, KC, F], BF16))
        w2f_b = e(nc.sbuf_tensor("w2f_b", [128, FC, D], BF16))
        b1_b = e(nc.sbuf_tensor("b1_b", [1, D], BF16))
        b23_b = e(nc.sbuf_tensor("b23_b", [1, 32], BF16))
        b1f_p = e(nc.sbuf_tensor("b1f_p", [128, FC], F32))
        b1fc = e(nc.sbuf_tensor("b1fc", [128, FC], F32))
        b2f_b = e(nc.sbuf_tensor("b2f_b", [1, D], BF16))

        ident_b = e(nc.sbuf_tensor("ident_b", [128, 128], BF16))
        ltinc_b = e(nc.sbuf_tensor("ltinc_b", [128, 128], BF16))
        ones_b = e(nc.sbuf_tensor("ones_b", [1, 128], BF16))
        twos_b = e(nc.sbuf_tensor("twos_b", [1, 128], BF16))
        onesf_r = e(nc.sbuf_tensor("onesf_r", [1, 128], F32))
        ones_cb = e(nc.sbuf_tensor("ones_cb", [128, 1], BF16))
        ones_cf = e(nc.sbuf_tensor("ones_cf", [128, 1], F32))
        CONSTS = [LN_EPS, -0.5, -1.0, 1.0, 0.5, GELU_TANH_SCALE,
                  GELU_C2, 2.0, 0.25, 0.0, GELU_C2 ** 0.5]
        cpool = e(nc.sbuf_tensor("cpool", [128, len(CONSTS)], F32))
        for ci, cv in enumerate(CONSTS):
            nc.const_aps.register(F32, cv, cpool[:, ci:ci + 1])
        stat2 = e(nc.sbuf_tensor("stat2", [128, NT, 2], F32))
        rstd_a = e(nc.sbuf_tensor("rstd_a", [128, NT], F32))
        scores = e(nc.sbuf_tensor("scores", [128, NT], F32))
        wpre = e(nc.sbuf_tensor("wpre", [128, NT], F32))
        st3s = e(nc.sbuf_tensor("st3s", [128, NT, 3], F32))
        tid_i = e(nc.sbuf_tensor("tid_i", [128, NT], I32))
        biga = e(nc.sbuf_tensor("biga", [128, NT], F32))
        rowb = e(nc.sbuf_tensor("rowb", [128, NT], F32))
        rank_i = e(nc.sbuf_tensor("rank_i", [128, NT], I32))
        imap_s = e(nc.sbuf_tensor("imap_s", [128, CNT], I32))
        statc3 = e(nc.sbuf_tensor("statc3", [128, 2, 3], F32))
        big2 = rank_i[:, 0:CNT]
        if NT >= 32:
            prod_t = wpre[:, 0:32].rearrange("p (a b) -> p a b", a=2)
        else:
            prod_t = e(nc.sbuf_tensor("prod_t", [128, 2, 16], F32))
        x_slab = e(nc.sbuf_tensor("x_slab", [128, 2, D], F32))
        xn_slab = e(nc.sbuf_tensor("xn_slab", [128, 2, D], F32))
        s_slab = e(nc.sbuf_tensor("s_slab", [128, 2, D], F32))
        xnb_c = e(nc.sbuf_tensor("xnb_c", [128, D], BF16))
        xnT_b = e(nc.sbuf_tensor("xnT_b", [128, 2, KC, 128], BF16))
        A_t = e(nc.sbuf_tensor("A_t", [128, 2, D], F32))
        B_t = e(nc.sbuf_tensor("B_t", [128, 2, D], F32))
        A0 = A_t[:, 0, :]
        B0 = B_t[:, 0, :]
        wr_row = A0[0:1, 0:D]
        V_t = e(nc.sbuf_tensor("V_t", [128, D], BF16))
        bn6 = e(nc.sbuf_tensor("bn6", [128, 12], F32))
        ln_t = e(nc.sbuf_tensor("ln_t", [128, 2], F32))
        bc_s = e(nc.sbuf_tensor("bc_s", [128, 2, 32], F32))
        cb_t = e(nc.sbuf_tensor("cb_t", [128, 2], F32))
        hT_g = [e(nc.sbuf_tensor(f"hT_g{j}", [128, GFC, 128], BF16))
                for j in range(3)]
        maskbf = xnb_c[:, 0:NT]
        mask_a = B0[:, 0:NT]
        rank_f = B0[:, NT:2 * NT]
        tmp_r = B0[:, 2 * NT:3 * NT]
        tmp_r2 = B0[:, 3 * NT:4 * NT]
        th_t = B0[:, 4 * NT:4 * NT + 2 * NT]
        lo_t = A0[0:1, 0:ROWS]
        hi_t = A0[0:1, ROWS:2 * ROWS]
        m_t = A0[0:1, 2 * ROWS:3 * ROWS]
        d_t = A0[0:1, 3 * ROWS:4 * ROWS]
        ge_t = A0[0:1, 4 * ROWS:5 * ROWS]
        mb_s = A0[:, 16:16 + ROWS]
        pcnt = A0[:, 24:24 + ROWS]
        tot_r = A0[0:1, 512:512 + NT]
        offsA = A0[0:1, 576:576 + NT]
        offsB = A0[0:1, 640:640 + NT]
        X_g = [A_t[:, 0, 0:GW], A_t[:, 1, 0:GW]]
        Y_g = [B_t[:, 0, 0:GW], B_t[:, 1, 0:GW]]
        V_g = [V_t[:, 0:GW], V_t[:, GW:2 * GW]]
        psA = e(nc.psum_tensor("psA", [128, D], F32))
        psB = e(nc.psum_tensor("psB", [128, D], F32))
        psC = e(nc.psum_tensor("psC", [128, D], F32))
        psD = e(nc.psum_tensor("psD", [128, D], BF16))
        psE = e(nc.psum_tensor("psE", [128, 512], F32))
        sem_names = ["x0", "x1", "ws", "wm", "wb", "pe", "dv", "ac", "gq",
                     "st0", "st1", "sd0", "sd1", "ga", "gb", "im", "sst",
                     "sc0", "sc1", "ix", "sx0", "sx1", "h0", "h1"]
        SEM = {n: e(nc.semaphore("s_" + n)) for n in sem_names}
        block = e(nc.Block())

        ops = {k: [] for k in ("sync", "gpsimd", "tensor", "vector", "scalar")}
        ctr = {n: 0 for n in sem_names}
        ENG_SEM = {"vector": ("dv", SEM["dv"]), "scalar": ("ac", SEM["ac"]),
                   "tensor": ("pe", SEM["pe"]), "gpsimd": ("gq", SEM["gq"])}
        M = {}

        def em(eng, fn):
            ops[eng].append(fn)

        def inc(name, n=1):
            ctr[name] += n
            return ctr[name]

        def wait(eng, name, val):
            if val > 0:
                em(eng, lambda E, s=SEM[name], v=val: E.wait_ge(s, v))

        def waitd(eng, name, key):
            em(eng, lambda E, s=SEM[name], k=key: E.wait_ge(s, M[k]))

        def op(eng, fn, tag=None):
            nm, sem = ENG_SEM[eng]
            em(eng, lambda E, f=fn, S=sem: f(E).then_inc(S, 1))
            v = inc(nm)
            if tag is not None:
                M[tag] = v
            return v

        def opq(eng, fn):
            em(eng, fn)

        def dma(eng, sem_name, fn):
            em(eng, lambda E, f=fn, S=SEM[sem_name]: f(E).then_inc(S, 16))
            return inc(sem_name, 16)

        REGS = {}

        def g_setup(E):
            rt = E.alloc_register()
            E.reg_mov(rt, T - 1)
            REGS["T"] = rt
            rc = E.alloc_register()
            E.reg_mov(rc, CT - 1)
            REGS["CT"] = rc
            E.memset(ident_b[:], 1.0)
            E.drain()
            E.affine_select(out=ident_b[:], in_=ident_b[:], pattern=[[-1, 128]],
                            compare_op=ALU.is_equal, fill=0.0,
                            base=0, channel_multiplier=1)
            E.memset(ltinc_b[:], 1.0)
            E.drain()
            E.affine_select(out=ltinc_b[:], in_=ltinc_b[:], pattern=[[1, 128]],
                            compare_op=ALU.is_gt, fill=0.0,
                            base=1, channel_multiplier=-1)
            E.memset(ones_b[:], 1.0)
            E.memset(twos_b[:], 2.0)
            E.memset(onesf_r[:], 1.0)
            E.memset(ones_cb[:], 1.0)
            E.memset(ones_cf[:], 1.0)
            for ci, cv in enumerate(CONSTS):
                E.memset(cpool[:, ci:ci + 1], cv)
            E.iota(tid_i[:], pattern=[[128, NT]], base=0, channel_multiplier=1)
            E.iota(biga[:], pattern=[[128, NT]], base=int(BIGV),
                   channel_multiplier=1, allow_small_or_imprecise_dtypes=True)
            E.iota(rowb[:], pattern=[[KTGT, ROWS], [0, RC]], base=0,
                   channel_multiplier=0, allow_small_or_imprecise_dtypes=True)
            E.iota(big2[:], pattern=[[128, CNT]], base=2 * int(BIGV),
                   channel_multiplier=1)
            return E.drain()
        op("gpsimd", g_setup, "setup")
        wait("scalar", "gq", M["setup"])
        wait("vector", "gq", M["setup"])
        wait("tensor", "gq", M["setup"])

        dma("gpsimd", "im", lambda E: E.dma_start(
            out=imap_d.rearrange("(a b) o -> a (b o)", a=128), in_=big2[:]))
        IM_PREFILL = ctr["im"]

        small_loads = [
            (wr_row[:], wr1_d[:]),
            (w23_b[:], w23_d.rearrange("(c p) n -> p c n", p=128)),
            (b1_b[:], b1_d[:]),
            (b23_b[:], b23_d[:]),
            (b1f_p[:], b1fp_d[:]),
            (b2f_b[:], b2f_d[:]),
        ]
        for dst, src in small_loads:
            dma("gpsimd", "wm", lambda E, d=dst, s=src: E.dma_start(
                out=d, in_=s))
        W_SMALL = ctr["wm"]
        dma("gpsimd", "wb", lambda E: E.dma_start(
            out=w1_b[:], in_=w1_d.rearrange("(c p) n -> p c n", p=128)))
        dma("gpsimd", "wb", lambda E: E.dma_start(
            out=w1f_b[:], in_=w1f_d.rearrange("(c p) n -> p c n", p=128)))
        W_P12 = ctr["wb"]

        wait("vector", "wm", W_SMALL)
        op("vector", lambda E: E.tensor_scalar_mul(
            b1fc[:], b1f_p[:], GELU_C2 ** 0.5), "b1fc")

        wait("tensor", "wm", W_SMALL)
        opq("tensor", lambda E: E.matmul(psB[:, 0:512], onesf_r[0:1, :],
                                         wr_row[0:1, 0:512],
                                         start=True, stop=True))
        op("tensor", lambda E: E.matmul(psB[:, 512:1024], onesf_r[0:1, :],
                                        wr_row[0:1, 512:1024],
                                        start=True, stop=True), "wrbc")

        wait("tensor", "wb", W_P12)

        def p12_s1(i):
            par = i % 2
            xpar = "x%d" % par
            xc = x_slab[:, par, :]
            if i >= 2:
                wait("sync", "sx%d" % par, 16 * ((i - 2) // 2 + 1))
            dma("sync", "h%d" % par, lambda E, ii=i, b=xc: E.dma_start(
                out=b[:, 0:512], in_=x_d[128 * ii:128 * (ii + 1), 0:512]))
            xh = ctr["h%d" % par]
            dma("sync", xpar, lambda E, ii=i, b=xc: E.dma_start(
                out=b[:, 512:1024], in_=x_d[128 * ii:128 * (ii + 1),
                                            512:1024]))
            xv = ctr[xpar]
            wait("vector", "h%d" % par, xh)
            if i >= 1:
                wait("vector", "dv", M[("bn", i - 1)])
            opq("vector", lambda E, b=xc: E.bn_stats(
                bn6[:, 0:6], b[:, 0:512]))
            wait("vector", xpar, xv)
            op("vector", lambda E, b=xc: E.bn_stats(
                bn6[:, 6:12], b[:, 512:1024]), ("bns", i))
            wait("vector", "dv", M[("bns", i)])
            op("vector", lambda E, ii=i: E.bn_aggr(stat2[:, ii, :], bn6[:]),
               ("bn", i))

        def p12_s2(i):
            par = i % 2
            xc = x_slab[:, par, :]
            xnc = xn_slab[:, par, :]
            Bp = B_t[:, par, :]
            psP = psA if par == 0 else psC
            bce0 = 0 if par == 0 else 256

            wait("vector", "ac", M[("rstd", i)])
            if i >= 2:
                wait("vector", "ac", M[("Ssil", i - 2)])
                wait("vector", "dv", M[("s8", i - 2)])
            op("vector", lambda E, b=xc, o=xnc, ii=i: E.tensor_scalar(
                out=o, in0=b, scalar1=stat2[:, ii, 0:1],
                scalar2=rstd_a[:, ii:ii + 1], op0=ALU.subtract, op1=ALU.mult),
               ("xn", i))
            wait("sync", "dv", M[("xn", i)])
            dma("sync", "sx%d" % par, lambda E, ii=i, b=xc: E.dma_start(
                out=out_d[128 * ii:128 * (ii + 1), :], in_=b))

            wait("vector", "dv", M[("xn", i)])
            if i >= 1:
                wait("vector", "pe", M[("btr", i - 1)])
            op("vector", lambda E, o=xnc: E.tensor_copy(xnb_c[:], o),
               ("cast", i))
            wait("tensor", "dv", M[("cast", i)])
            for k in range(KC):
                fn = (lambda E, kk=k: E.transpose(
                    psD[:, 128 * kk:128 * (kk + 1)],
                    xnb_c[:, 128 * kk:128 * (kk + 1)], ident_b[:]))
                if k == KC - 1:
                    op("tensor", fn, ("btr", i))
                else:
                    opq("tensor", lambda E, f=fn: f(E))
            if i == 0:
                wait("vector", "pe", M["wrbc"])
            op("vector", lambda E, o=xnc, ii=i: E.scalar_tensor_tensor(
                out=V_t[:], in0=o, scalar=1.0, in1=psB[:, 0:D],
                op0=ALU.mult, op1=ALU.mult,
                accum_out=scores[:, ii:ii + 1]), ("scr", i))
            wait("vector", "pe", M[("btr", i)])
            op("vector", lambda E, ii=i: E.tensor_copy(
                xnT_b[:, ii % 2, :, :],
                psD[:, 0:D].rearrange("p (k t) -> p k t", k=KC)),
               ("tev", i))

            wait("tensor", "dv", M[("tev", i)])
            if i >= 2:
                wait("tensor", "ac", M[("spl", i - 2)])
                wait("tensor", "dv", M[("s8", i - 2)])
            for h, n0, w in halves:
                opq("tensor", lambda E, nn=n0, ww=w, P=psP: E.matmul(
                    P[:, nn:nn + ww], ones_b[:], b1_b[0:1, nn:nn + ww],
                    start=True, stop=False))
                for k in range(KC):
                    fn = (lambda E, ii=i, kk=k, nn=n0, ww=w, P=psP:
                          E.matmul(P[:, nn:nn + ww], xnT_b[:, ii % 2, kk, :],
                                   w1_b[:, kk, nn:nn + ww],
                                   start=False, stop=(kk == KC - 1)))
                    if h == len(halves) - 1 and k == KC - 1:
                        op("tensor", fn, ("dmm", i))
                    else:
                        opq("tensor", lambda E, f=fn: f(E))
            if i >= 2:
                wait("tensor", "dv", M[("bce", i - 2)])
            opq("tensor", lambda E, b0=bce0: E.matmul(
                psE[:, b0:b0 + 32], ones_b[:], b23_b[:],
                start=True, stop=False))
            for k in range(KC):
                fn = (lambda E, ii=i, kk=k, b0=bce0: E.matmul(
                    psE[:, b0:b0 + 32], xnT_b[:, ii % 2, kk, :],
                    w23_b[:, kk, :], start=False, stop=(kk == KC - 1)))
                if k == KC - 1:
                    op("tensor", fn, ("bcm", i))
                else:
                    opq("tensor", lambda E, f=fn: f(E))

        def p12_s3(i):
            par = i % 2
            xc = x_slab[:, par, :]
            xnc = xn_slab[:, par, :]
            sc_ = s_slab[:, par, :]
            Ap = A_t[:, par, :]
            Bp = B_t[:, par, :]
            psP = psA if par == 0 else psC
            bce0 = 0 if par == 0 else 256

            wait("scalar", "pe", M[("dmm", i)])
            if i >= 2:
                wait("scalar", "dv", M[("q3", i - 2)])
            op("scalar", lambda E, P=psP, Bb=Bp: E.activation(
                out=Bb, in_=P[:, 0:D], func=AF.Exp), ("spe", i))
            wait("scalar", "ac", M[("spe", i)])
            if i >= 2:
                wait("scalar", "st%d" % par,
                     16 * ((i - 2) // 2 + 1))
            op("scalar", lambda E, Aa=Ap, Bb=Bp: E.activation(
                out=Aa, in_=Bb, func=AF.Ln, bias=1.0), ("spl", i))
            M[("dstt", i)] = M[("spl", i)]
            wait("scalar", "ac", M[("spl", i)])
            wait("scalar", "dv", M[("scr", i)])
            op("scalar", lambda E, o=xnc, Bb=Bp: E.activation(
                out=Bb, in_=o, func=AF.Silu), ("Ssil", i))
            wait("vector", "pe", M[("bcm", i)])
            op("vector", lambda E, ii=i, b0=bce0: E.tensor_copy(
                bc_s[:, ii % 2, :], psE[:, b0:b0 + 32]), ("bce", i))
            wait("vector", "dv", M[("bce", i)])
            op("vector", lambda E, ii=i: E.tensor_mul(
                prod_t[:, ii % 2, :], bc_s[:, ii % 2, 0:16],
                bc_s[:, ii % 2, 16:32]), None)
            wait("vector", "dv", ctr["dv"])
            op("vector", lambda E, ii=i: E.reduce_sum(
                cb_t[:, ii % 2:ii % 2 + 1], prod_t[:, ii % 2, :],
                axis=mybir.AxisListType.X), ("cb", i))
            wait("vector", "dv", M[("cb", i)])
            wait("vector", "ac", M[("spl", i)])
            if i >= 2:
                wait("vector", "sd%d" % par, 16 * ((i - 2) // 2 + 1))
                wait("vector", "dv", M[("q3", i - 2)])
            op("vector", lambda E, o=xnc, s2=sc_, ii=i, Aa=Ap:
               E.scalar_tensor_tensor(
                   out=s2, in0=o, scalar=cb_t[:, ii % 2:ii % 2 + 1], in1=Aa,
                   op0=ALU.mult, op1=ALU.mult), ("s8", i))
            wait("gpsimd", "dv", M[("s8", i)])
            dma("gpsimd", "sd%d" % par, lambda E, ii=i, s2=sc_: E.dma_start(
                out=s_d[128 * ii:128 * (ii + 1), :], in_=s2))

            wait("vector", "ac", M[("Ssil", i)])
            wait("vector", "dv", M[("s8", i)])
            op("vector", lambda E, s2=sc_, Aa=Ap, Bb=Bp: E.tensor_mul(
                Aa, s2, Bb), ("q3", i))
            wait("gpsimd", "sx%d" % par, 16 * (i // 2 + 1))
            wait("gpsimd", "dv", M[("q3", i)])
            dma("gpsimd", "st%d" % par, lambda E, ii=i, Aa=Ap: E.dma_start(
                out=out_d[128 * ii:128 * (ii + 1), :], in_=Aa,
                accum_op=ALU.add))
            i2 = i + 2
            if i2 < NT:
                wait("scalar", "dv", M[("bn", i2)])
                op("scalar", lambda E, ii=i2: E.activation(
                    out=ln_t[:, ii % 2:ii % 2 + 1], in_=stat2[:, ii, 1:2],
                    func=AF.Ln, bias=LN_EPS), ("lnv", i2))
                wait("scalar", "ac", M[("lnv", i2)])
                op("scalar", lambda E, ii=i2: E.activation(
                    out=rstd_a[:, ii:ii + 1],
                    in_=ln_t[:, ii % 2:ii % 2 + 1],
                    func=AF.Exp, scale=-0.5), ("rstd", i2))

        p12_s1(0)
        p12_s1(1)
        for i0 in (0, 1):
            wait("scalar", "dv", M[("bn", i0)])
            op("scalar", lambda E, ii=i0: E.activation(
                out=ln_t[:, ii % 2:ii % 2 + 1], in_=stat2[:, ii, 1:2],
                func=AF.Ln, bias=LN_EPS), ("lnv", i0))
            wait("scalar", "ac", M[("lnv", i0)])
            op("scalar", lambda E, ii=i0: E.activation(
                out=rstd_a[:, ii:ii + 1], in_=ln_t[:, ii % 2:ii % 2 + 1],
                func=AF.Exp, scale=-0.5), ("rstd", i0))
        p12_s2(0)
        for i in range(NT):
            if i + 2 < NT:
                p12_s1(i + 2)
            if i + 1 < NT:
                p12_s2(i + 1)
            p12_s3(i)

        ST_TOT = {0: ctr["st0"], 1: ctr["st1"]}
        SD_TOT = (ctr["sd0"], ctr["sd1"])

        dma("gpsimd", "wb", lambda E: E.dma_start(
            out=w2f_b[:], in_=w2f_d.rearrange("(c p) n -> p c n", p=128)))
        W_ALL = ctr["wb"]

        wait("scalar", "dv", M[("scr", NT - 1)])
        wait("scalar", "dv", M[("cb", NT - 1)])
        op("scalar", lambda E: E.activation(
            out=wpre[:], in_=scores[:], func=AF.Exp, scale=-1.0), "wpe")
        wait("scalar", "ac", M["wpe"])
        op("scalar", lambda E: E.activation(
            out=wpre[:], in_=wpre[:], func=AF.Identity, bias=1.0), "wp1")
        wait("vector", "ac", M["wp1"])
        op("vector", lambda E: E.reciprocal(wpre[:], wpre[:]), "wpr")
        wait("vector", "dv", M["wpr"])
        op("vector", lambda E: E.tensor_scalar_mul(wpre[:], wpre[:], 0.5),
           "wps")

        wait("vector", "dv", M["wps"])
        opq("vector", lambda E: E.tensor_copy(st3s[:, :, 0],
                                              stat2[:, :, 0]))
        opq("vector", lambda E: E.tensor_copy(st3s[:, :, 1], rstd_a[:]))
        op("vector", lambda E: E.tensor_copy(st3s[:, :, 2], wpre[:]),
           "st3c")
        wait("sync", "dv", M["st3c"])
        NT2 = NT // 2
        for hh in range(2):
            dma("sync", "sst", lambda E, h2=hh: E.dma_start(
                out=st3_d[128 * NT2 * h2:128 * NT2 * (h2 + 1), :].rearrange(
                    "(i p) k -> p i k", p=128),
                in_=st3s[:, NT2 * h2:NT2 * (h2 + 1), :],
                allow_slow_non_contiguous=True))
        SST_TOT = ctr["sst"]

        wait("vector", "dv", ctr["dv"])
        wait("vector", "ac", M[("Ssil", NT - 1)])
        wait("vector", "st0", ST_TOT[0])
        wait("vector", "st1", ST_TOT[1])
        op("vector", lambda E: E.memset(lo_t, -LOHI), None)
        op("vector", lambda E: E.memset(hi_t, LOHI), "binit")

        for j in range(BI):
            wait("vector", "dv", ctr["dv"])
            op("vector", lambda E: E.tensor_add(m_t, lo_t, hi_t), None)
            wait("vector", "dv", ctr["dv"])
            op("vector", lambda E: E.tensor_scalar_mul(m_t, m_t, 0.5),
               ("bm", j))
            wait("tensor", "dv", M[("bm", j)])
            op("tensor", lambda E: E.matmul(psE[:, 0:ROWS], onesf_r[:], m_t,
                                            start=True, stop=True), ("bmb", j))
            wait("vector", "pe", M[("bmb", j)])
            for r in range(ROWS):
                fn = (lambda E, rr=r: E.tensor_scalar(
                    out=th_t[:, rr * RC:(rr + 1) * RC],
                    in0=scores[:, rr * RC:(rr + 1) * RC],
                    scalar1=psE[:, rr:rr + 1], scalar2=0.0,
                    op0=ALU.is_gt, op1=ALU.add,
                    accum_out=pcnt[:, rr:rr + 1]))
                if r == ROWS - 1:
                    op("vector", fn, ("bcmp", j))
                else:
                    opq("vector", lambda E, f=fn: f(E))
            wait("tensor", "dv", M[("bcmp", j)])
            op("tensor", lambda E: E.matmul(psE[0:1, 4:4 + ROWS], ones_cf[:],
                                            pcnt, start=True, stop=True),
               ("bcnt", j))
            wait("vector", "pe", M[("bcnt", j)])
            op("vector", lambda E: E.tensor_scalar(
                out=ge_t, in0=psE[0:1, 4:4 + ROWS], scalar1=float(KTGT),
                scalar2=None, op0=ALU.is_ge), None)
            op("vector", lambda E: E.tensor_sub(d_t, m_t, lo_t), None)
            wait("vector", "dv", ctr["dv"])
            op("vector", lambda E: E.tensor_mul(d_t, d_t, ge_t), None)
            wait("vector", "dv", ctr["dv"])
            op("vector", lambda E: E.tensor_add(lo_t, lo_t, d_t), None)
            wait("vector", "dv", ctr["dv"])
            op("vector", lambda E: E.tensor_sub(d_t, hi_t, m_t), None)
            wait("vector", "dv", ctr["dv"])
            op("vector", lambda E: E.tensor_mul(d_t, d_t, ge_t), None)
            wait("vector", "dv", ctr["dv"])
            op("vector", lambda E: E.tensor_add(hi_t, m_t, d_t), ("bupd", j))

        wait("tensor", "dv", M[("bupd", BI - 1)])
        op("tensor", lambda E: E.matmul(psE[:, 0:ROWS], onesf_r[:], lo_t,
                                        start=True, stop=True), "bmaskmb")
        wait("vector", "pe", M["bmaskmb"])
        op("vector", lambda E: E.tensor_copy(mb_s, psE[:, 0:ROWS]), "bmbc2")
        wait("vector", "dv", M["bmbc2"])
        for r in range(ROWS):
            fn = (lambda E, rr=r: E.tensor_scalar(
                out=mask_a[:, rr * RC:(rr + 1) * RC],
                in0=scores[:, rr * RC:(rr + 1) * RC],
                scalar1=mb_s[:, rr:rr + 1], scalar2=None, op0=ALU.is_gt))
            if r == ROWS - 1:
                op("vector", fn, "bmask")
            else:
                opq("vector", lambda E, f=fn: f(E))

        wait("vector", "dv", M["bmask"])
        wait("vector", "pe", M[("btr", NT - 1)])
        op("vector", lambda E: E.tensor_copy(maskbf, mask_a), "mkb")
        wait("tensor", "dv", M["mkb"])
        op("tensor", lambda E: E.matmul(psE[:, 256:256 + NT], ltinc_b[:],
                                        maskbf, start=True, stop=True),
           "incl")
        op("tensor", lambda E: E.matmul(psE[0:1, 320:320 + NT], ones_cb[:],
                                        maskbf, start=True, stop=True), "totm")
        wait("vector", "pe", M["totm"])
        op("vector", lambda E: E.tensor_copy(rank_f, psE[:, 256:256 + NT]),
           "rk0")
        op("vector", lambda E: E.tensor_copy(tot_r, psE[0:1, 320:320 + NT]),
           "totc")
        wait("vector", "dv", M["totc"])
        for r in range(ROWS):
            a = offsA[0:1, r * RC:(r + 1) * RC]
            b = offsB[0:1, r * RC:(r + 1) * RC]
            t = tot_r[0:1, r * RC:(r + 1) * RC]
            opq("vector", lambda E, aa=a: E.memset(aa[0:1, 0:1], 0.0))
            op("vector", lambda E, aa=a, tt=t: E.tensor_copy(
                aa[0:1, 1:RC], tt[0:1, 0:RC - 1]), ("cse", r, 0))
            src, dst = a, b
            kk_ = 1
            step = 0
            while kk_ < RC:
                wait("vector", "dv", M[("cse", r, step)])
                opq("vector", lambda E, s3=src, d3=dst, k4=kk_: E.tensor_copy(
                    d3[0:1, 0:k4], s3[0:1, 0:k4]))
                op("vector", lambda E, s3=src, d3=dst, k4=kk_: E.tensor_add(
                    d3[0:1, k4:RC], s3[0:1, k4:RC], s3[0:1, 0:RC - k4]),
                   ("cse", r, step + 1))
                src, dst = dst, src
                kk_ *= 2
                step += 1
            if src is not offsA:
                wait("vector", "dv", M[("cse", r, step)])
                op("vector", lambda E, s3=src, aa=a: E.tensor_copy(aa, s3),
                   ("cse", r, step + 1))
            M[("csef", r)] = ctr["dv"]
        wait("tensor", "dv", max(M[("csef", r)] for r in range(ROWS)))
        op("tensor", lambda E: E.matmul(psE[:, 384:384 + NT], onesf_r[:],
                                        offsA[0:1, 0:NT], start=True,
                                        stop=True), "offb")
        wait("vector", "pe", M["offb"])
        op("vector", lambda E: E.tensor_sub(rank_f, rank_f, mask_a), None)
        wait("vector", "dv", ctr["dv"])
        op("vector", lambda E: E.tensor_add(rank_f, rank_f,
                                            psE[:, 384:384 + NT]), "rk1")
        wait("vector", "dv", M["rk1"])
        op("vector", lambda E: E.tensor_scalar(
            out=tmp_r, in0=rank_f, scalar1=float(KTGT), scalar2=None,
            op0=ALU.is_ge), None)
        op("vector", lambda E: E.tensor_scalar(
            out=tmp_r2, in0=mask_a, scalar1=-1.0, scalar2=1.0,
            op0=ALU.mult, op1=ALU.add), None)
        wait("vector", "dv", ctr["dv"])
        op("vector", lambda E: E.tensor_add(tmp_r, tmp_r, tmp_r2), None)
        wait("vector", "dv", ctr["dv"])
        op("vector", lambda E: E.tensor_mul(tmp_r, tmp_r, biga[:]), None)
        wait("vector", "dv", ctr["dv"])
        op("vector", lambda E: E.tensor_add(rank_f, rank_f, tmp_r), None)
        wait("vector", "dv", ctr["dv"])
        op("vector", lambda E: E.tensor_add(rank_f, rank_f, rowb[:]), "rk2")
        wait("vector", "dv", M["rk2"])
        wait("vector", "im", IM_PREFILL)
        op("vector", lambda E: E.tensor_copy(rank_i[:], rank_f), "rki")

        def emit_im_scatter(c):
            dma("gpsimd", "im", lambda E, cc=c: E.indirect_dma_start(
                out=imap_d, out_offset=IndirectOffsetOnAxis(
                    ap=rank_i[:, cc:cc + 1], axis=0),
                in_=tid_i[:, cc:cc + 1], in_offset=None,
                bounds_check=REGS["CT"], oob_is_err=False))
        wait("gpsimd", "dv", M["rki"])
        wait("gpsimd", "im", IM_PREFILL)
        for c in range(RC):
            emit_im_scatter(c)
        IM_H1 = ctr["im"]
        wait("sync", "im", IM_H1)
        dma("sync", "ix", lambda E: E.dma_start(
            out=imap_s[:, 0:CNT // 2],
            in_=imap_d[0:CT // 2, :].rearrange("(c p) o -> p (c o)", p=128),
            allow_slow_non_contiguous=True))
        IMS_V = ctr["ix"]

        wait("tensor", "wb", W_ALL)
        wait("gpsimd", "ix", IMS_V)
        wait("gpsimd", "sd0", SD_TOT[0])
        wait("gpsimd", "sd1", SD_TOT[1])
        wait("gpsimd", "sst", SST_TOT)

        def emit_gathers(j):
            c = j % 2
            gpar = "ga" if c == 0 else "gb"
            if j >= CNT // 2:
                wait("gpsimd", "ix", 32)
            if j >= 2:
                wait("gpsimd", "dv", M[("Vo", j - 2)])
                wait("gpsimd", "sc%d" % (j % 2),
                     16 * ((j - 2) // 2 + 1))
            dma("gpsimd", gpar, lambda E, jj=j, cc=c: E.indirect_dma_start(
                out=x_slab[:, cc, :], out_offset=None, in_=x_d,
                in_offset=IndirectOffsetOnAxis(
                    ap=imap_s[:, jj:jj + 1], axis=0),
                bounds_check=REGS["T"], oob_is_err=False))
            dma("gpsimd", gpar, lambda E, jj=j, cc=c: E.indirect_dma_start(
                out=s_slab[:, cc, :], out_offset=None, in_=s_d,
                in_offset=IndirectOffsetOnAxis(
                    ap=imap_s[:, jj:jj + 1], axis=0),
                bounds_check=REGS["T"], oob_is_err=False))
            dma("gpsimd", gpar, lambda E, jj=j, cc=c: E.indirect_dma_start(
                out=statc3[:, cc, :], out_offset=None, in_=st3_d,
                in_offset=IndirectOffsetOnAxis(
                    ap=imap_s[:, jj:jj + 1], axis=0),
                bounds_check=REGS["T"], oob_is_err=False))
            M[("gat", j)] = ctr[gpar]

        PB2 = [psB[:, 0:512], psB[:, 512:1024], psE[:, 0:512]]
        VFE = {}

        def emit_head_v(j):
            c = j % 2
            gpar = "ga" if c == 0 else "gb"
            xc = x_slab[:, c, :]
            xnc = xn_slab[:, c, :]

            wait("vector", gpar, M[("gat", j)])
            if j < 2:
                wait("vector", "st%d" % (j % 2), ST_TOT[j % 2])
            else:
                wait("vector", "dv", M[("Vo", j - 2)])
            op("vector", lambda E, b=xc, o=xnc, cc=c: E.tensor_scalar(
                out=o, in0=b, scalar1=statc3[:, cc, 0:1],
                scalar2=statc3[:, cc, 1:2], op0=ALU.subtract, op1=ALU.mult),
               ("xnb2", j))
            wait("vector", "dv", M[("xnb2", j)])
            if j >= 1:
                wait("vector", "pe", M[("btr2", j - 1)])
            op("vector", lambda E, o=xnc: E.tensor_copy(xnb_c[:], o),
               ("cst2", j))

        def emit_head_pe(j):
            wait("tensor", "dv", M[("cst2", j)])
            for k in range(KC):
                fn = (lambda E, kk=k: E.transpose(
                    psD[:, 128 * kk:128 * (kk + 1)],
                    xnb_c[:, 128 * kk:128 * (kk + 1)], ident_b[:]))
                if k == KC - 1:
                    op("tensor", fn, ("btr2", j))
                else:
                    opq("tensor", lambda E, f=fn: f(E))
            wait("vector", "pe", M[("btr2", j)])
            op("vector", lambda E, jj=j: E.tensor_copy(
                xnT_b[:, jj % 2, :, :],
                psD[:, 0:D].rearrange("p (k t) -> p k t", k=KC)),
               ("tev2", j))

        def emit_body(j):
            c = j % 2
            xc = x_slab[:, c, :]
            xnc = xn_slab[:, c, :]
            sc_ = s_slab[:, c, :]
            psM = psC if c == 0 else psA

            def emit_m1(g):
                G = NG * j + g
                pb = PB2[G % 3].rearrange("p (j t) -> p j t", j=GFC)
                if g == 0:
                    wait("tensor", "dv", M[("tev2", j)])
                if G >= 3:
                    pg = divmod(G - 3, NG)
                    wait("tensor", VFE[pg], M[("vfree",) + pg])
                for jj in range(GFC):
                    fc = GFC * g + jj
                    for k in range(KC):
                        fn = (lambda E, f=fc, j2=jj, kk=k, P=pb, jc=j:
                              E.matmul(P[:, j2, :],
                                       w1f_b[:, kk, 128 * f:128 * (f + 1)],
                                       xnT_b[:, jc % 2, kk, :],
                                       start=(kk == 0), stop=(kk == KC - 1)))
                        if jj == GFC - 1 and k == KC - 1:
                            op("tensor", fn, ("m1", j, g))
                        else:
                            opq("tensor", lambda E, f=fn: f(E))

            def emit_evac(g):
                G = NG * j + g
                b = G % 2
                bk = G % 3
                eng = "vector"
                VFE[(j, g)] = "dv"
                wait(eng, "pe", M[("m1", j, g)])
                if G >= 2:
                    pg = divmod(G - 2, NG)
                    wait(eng, "ac", M[("sq",) + pg])
                    wait(eng, "dv", M[("h",) + pg])
                for jj in range(GFC):
                    fc = GFC * g + jj
                    if eng == "vector":
                        fn = (lambda E, bb=b, bkk=bk, j2=jj, f=fc:
                              E.tensor_scalar(
                                  out=V_t[:, bb * GW + j2 * 128:
                                          bb * GW + (j2 + 1) * 128],
                                  in0=PB2[bkk][:, j2 * 128:(j2 + 1) * 128],
                                  scalar1=b1f_p[:, f:f + 1], scalar2=None,
                                  op0=ALU.add))
                    else:
                        fn = (lambda E, bb=b, bkk=bk, j2=jj, f=fc:
                              E.activation(
                                  out=V_t[:, bb * GW + j2 * 128:
                                          bb * GW + (j2 + 1) * 128],
                                  in_=PB2[bkk][:, j2 * 128:(j2 + 1) * 128],
                                  func=AF.Identity, bias=b1f_p[:, f:f + 1]))
                    if jj == GFC - 1:
                        op(eng, fn, ("vfree", j, g))
                    else:
                        opq(eng, lambda E, f=fn: f(E))

            def emit_sq(g):
                G = NG * j + g
                b = G % 2
                wait("scalar", VFE[(j, g)], M[("vfree", j, g)])
                if G >= 2:
                    pg = divmod(G - 2, NG)
                    wait("scalar", "dv", M[("h",) + pg])
                op("scalar", lambda E, bb=b: E.activation(
                    out=X_g[bb], in_=V_g[bb], func=AF.Square,
                    scale=GELU_C2 ** 0.5), ("sq", j, g))

            def emit_u(gj, g):
                G = NG * gj + g
                b = G % 2
                wait("vector", "ac", M[("sq", gj, g)])
                if G >= 2:
                    pg = divmod(G - 2, NG)
                    wait("vector", "ac", M[("tanh",) + pg])
                if G >= 1 and g == 0:
                    wait("vector", "dv", M[("Vo", gj - 1)])
                op("vector", lambda E, bb=b: E.scalar_tensor_tensor(
                    out=Y_g[bb], in0=X_g[bb], scalar=1.0, in1=V_g[bb],
                    op0=ALU.add, op1=ALU.mult), ("u", gj, g))

            def emit_tanh(gj, g):
                wait("scalar", "dv", M[("u", gj, g)])
                op("scalar", lambda E, bb=(NG * gj + g) % 2: E.activation(
                    out=X_g[bb], in_=Y_g[bb], func=AF.Tanh,
                    scale=GELU_TANH_SCALE), ("tanh", gj, g))

            def emit_h(gj, g):
                G = NG * gj + g
                b = G % 2
                wait("vector", "ac", M[("tanh", gj, g)])
                if G >= 3:
                    pg = divmod(G - 3, NG)
                    wait("vector", "pe", M[("m2p",) + pg])
                op("vector", lambda E, bb=b, b3=G % 3: E.scalar_tensor_tensor(
                    out=hT_g[b3].rearrange("p j t -> p (j t)"),
                    in0=X_g[bb], scalar=1.0, in1=V_g[bb],
                    op0=ALU.add, op1=ALU.mult), ("h", gj, g))

            def emit_pbias():
                wait("tensor", "dv", M[("tev2", j)])
                if j >= 2:
                    wait("tensor", "dv", M[("Vy", j - 2)])
                for h, n0, w in halves:
                    fn = (lambda E, nn=n0, ww=w, P=psM: E.matmul(
                        P[:, nn:nn + ww], twos_b[:], b2f_b[0:1, nn:nn + ww],
                        start=True, stop=False))
                    if h == len(halves) - 1:
                        op("tensor", fn, ("pbias", j))
                    else:
                        opq("tensor", lambda E, f=fn: f(E))

            def emit_m2p(gj, g):
                G = NG * gj + g
                b = G % 3
                wait("tensor", "dv", M[("h", gj, g)])
                if gj == j and g == 0:
                    wait("tensor", "pe", M[("pbias", j)])
                n_mm = len(halves) * GFC
                mi = 0
                for h, n0, w in halves:
                    for jj in range(GFC):
                        fc = GFC * g + jj
                        mi += 1
                        stop = (g == NG - 1 and jj == GFC - 1)
                        fn = (lambda E, f=fc, j2=jj, nn=n0, ww=w,
                              P=(psC if gj % 2 == 0 else psA),
                              bb=b, st=stop: E.matmul(
                                  P[:, nn:nn + ww], hT_g[bb][:, j2, :],
                                  w2f_b[:, f, nn:nn + ww],
                                  start=False, stop=st))
                        if mi == n_mm:
                            op("tensor", fn, ("m2p", gj, g))
                        else:
                            opq("tensor", lambda E, f=fn: f(E))

            def back(k, d):
                gg = k - d
                if gg >= NG:
                    return None
                if gg >= 0:
                    return (j, gg)
                if j == 0:
                    return None
                return (j - 1, NG + gg)

            for k in range(NG + 4):
                if k < NG:
                    emit_m1(k)
                if k == 5 and j + 1 < CNT:
                    emit_head_v(j + 1)
                if k == 6 and j + 1 < CNT:
                    emit_head_pe(j + 1)
                t1 = back(k, 1)
                if k == 1 and j + 1 < CNT:
                    emit_gathers(j + 1)
                if t1 and t1[0] == j:
                    emit_u(*t1)
                    emit_tanh(*t1)
                t2 = back(k, 2)
                if t2 and t2[0] == j:
                    emit_h(*t2)
                if k == 2:
                    emit_pbias()
                t4 = back(k, 4)
                if t4 and t4[0] == j:
                    emit_m2p(*t4)
                if k < NG:
                    emit_evac(k)
                    emit_sq(k)
            M[("m2d", j)] = M[("m2p", j, NG - 1)]

            wait("vector", "pe", M[("m2d", j)])
            op("vector", lambda E, o=xnc, P=psM, cc=c: E.scalar_tensor_tensor(
                out=o, in0=P[:, 0:D], scalar=statc3[:, cc, 2:3], in1=o,
                op0=ALU.mult, op1=ALU.add), ("Vy", j))
            wait("scalar", "dv", M[("Vy", j)])
            op("scalar", lambda E, o=xnc: E.activation(
                out=B0, in_=o, func=AF.Silu), ("Sy", j))
            wait("vector", "ac", M[("Sy", j)])
            op("vector", lambda E, o=xnc, s2=sc_: E.tensor_mul(
                o, B0, s2), ("Vt", j))
            wait("vector", "dv", M[("Vt", j)])
            op("vector", lambda E, o=xnc, b=xc: E.tensor_add(b, b, o),
               ("Vo", j))
            if j == 0:
                wait("gpsimd", "st0", ST_TOT[0])
                wait("gpsimd", "st1", ST_TOT[1])
            wait("gpsimd", "dv", M[("Vo", j)])
            dma("gpsimd", "sc%d" % c, lambda E, jj=j, b=xc:
                E.indirect_dma_start(
                    out=out_d, out_offset=IndirectOffsetOnAxis(
                        ap=imap_s[:, jj:jj + 1], axis=0),
                    in_=b, in_offset=None,
                    bounds_check=REGS["T"], oob_is_err=False))

        emit_gathers(0)
        emit_head_v(0)
        emit_head_pe(0)
        rest = list(range(RC, NT))
        nb = max(1, CNT // 2 - 2)
        per = max(1, (len(rest) + nb - 1) // nb)
        for j in range(CNT):
            emit_body(j)
            if rest:
                for c in rest[:per]:
                    emit_im_scatter(c)
                rest = rest[per:]
                if not rest:
                    wait("sync", "im", ctr["im"])
                    dma("sync", "ix", lambda E: E.dma_start(
                        out=imap_s[:, CNT // 2:CNT],
                        in_=imap_d[CT // 2:CT, :].rearrange(
                            "(c p) o -> p (c o)", p=128),
                        allow_slow_non_contiguous=True))

        if cfg.get("DBG"):
            wait("gpsimd", "dv", M["rki"])
            def g_dbg(E):
                E.dma_start(out=dbg_d[:, 0, :], in_=scores[:]).then_inc(
                    SEM["im"], 16)
                E.dma_start(out=dbg_d[:, 1, :], in_=mask_a).then_inc(
                    SEM["im"], 16)
                E.dma_start(out=dbg_d[:, 2, :], in_=rank_f).then_inc(
                    SEM["im"], 16)
                E.dma_start(out=dbg_d[:, 3, :], in_=wpre[:]).then_inc(
                    SEM["im"], 16)
                E.dma_start(out=dbg_d[:, 4, :], in_=rstd_a[:]).then_inc(
                    SEM["im"], 16)
                E.dma_start(out=dbg_d[:, 5, :], in_=stat2[:, :, 0],
                            allow_slow_non_contiguous=True).then_inc(
                    SEM["im"], 16)
            em("gpsimd", g_dbg)
            ctr["im"] += 96

        wait("sync", "st0", ST_TOT[0])
        wait("sync", "st1", ST_TOT[1])
        wait("sync", "sc0", ctr["sc0"])
        wait("sync", "sc1", ctr["sc1"])
        if cfg.get("DBG"):
            wait("sync", "im", ctr["im"])

        @block.sync
        def _(E):
            for f in ops["sync"]:
                f(E)

        @block.gpsimd
        def _(E):
            for f in ops["gpsimd"]:
                f(E)

        @block.tensor
        def _(E):
            for f in ops["tensor"]:
                f(E)

        @block.vector
        def _(E):
            for f in ops["vector"]:
                f(E)

        @block.scalar
        def _(E):
            for f in ops["scalar"]:
                f(E)

    return nc


def _pack_inputs(core_x, W1, b1, W2, b2, W3, b3, wr, W1f, b1f, W2f, b2f, cfg):
    D, F = cfg["D"], cfg["F"]
    FC = F // 128
    return {
        "x": np.ascontiguousarray(core_x),
        "w1": np.ascontiguousarray(W1.T),
        "b1": np.ascontiguousarray(b1.reshape(1, -1)),
        "w23": np.ascontiguousarray(np.concatenate([W2.T, W3.T], axis=1)),
        "b23": np.ascontiguousarray(np.concatenate([b2, b3]).reshape(1, -1)),
        "wr1": np.ascontiguousarray(wr.reshape(1, -1)),
        "w1f": np.ascontiguousarray(W1f),
        "b1fp": np.ascontiguousarray(b1f.reshape(FC, 128).T),
        "w2f": np.ascontiguousarray(W2f),
        "b2f": np.ascontiguousarray(b2f.reshape(1, -1)),
    }


def kernel(L0_attn_x, L0_attn_W1, L0_attn_b1, L0_attn_W2, L0_attn_b2,
           L0_attn_W3, L0_attn_b3, L0_attn_A, L0_ffn_w_router, L0_ffn_W1,
           L0_ffn_b1, L0_ffn_W2, L0_ffn_b2):
    cfg = CFG_FULL
    B, L, Dm = L0_attn_x.shape
    n_cores = 8
    rows = B // n_cores
    x = np.asarray(L0_attn_x, dtype=np.float32)

    nc = build_core_program(cfg)
    in_maps = []
    for c in range(n_cores):
        shard = x[c * rows:(c + 1) * rows].reshape(rows * L, Dm)
        in_maps.append(_pack_inputs(
            shard, np.asarray(L0_attn_W1), np.asarray(L0_attn_b1),
            np.asarray(L0_attn_W2), np.asarray(L0_attn_b2),
            np.asarray(L0_attn_W3), np.asarray(L0_attn_b3),
            np.asarray(L0_ffn_w_router), np.asarray(L0_ffn_W1),
            np.asarray(L0_ffn_b1), np.asarray(L0_ffn_W2),
            np.asarray(L0_ffn_b2), cfg))
    res = run_bass_kernel_spmd(nc, in_maps, core_ids=list(range(n_cores)))
    outs = [r["out"].reshape(rows, L, Dm) for r in res.results]
    return np.concatenate(outs, axis=0)


# version 52
# speedup vs baseline: 1.0228x; 1.0228x over previous
import numpy as np
from contextlib import ExitStack

import concourse.bass as bass
import concourse.mybir as mybir
from concourse.bass import IndirectOffsetOnAxis
from concourse.bass_utils import run_bass_kernel_spmd

F32 = mybir.dt.float32
BF16 = mybir.dt.bfloat16
I32 = mybir.dt.int32
ALU = mybir.AluOpType
AF = mybir.ActivationFunctionType

GELU_C2 = 0.044715
GELU_TANH_SCALE = 0.7978845608028654
LN_EPS = 1e-5
BIGV = 1.0e6

CFG_FULL = dict(D=1024, F=4096, SEQ=4096, ROWS=2, BI=23, LOHI=1.0)


def build_core_program(cfg):
    D, F, SEQ, ROWS = cfg["D"], cfg["F"], cfg["SEQ"], cfg["ROWS"]
    BI, LOHI = cfg["BI"], cfg["LOHI"]
    KC = D // 128
    FC = F // 128
    T = ROWS * SEQ
    NT = T // 128
    RC = SEQ // 128
    KTGT = SEQ // 2
    CT = ROWS * KTGT
    CNT = CT // 128
    GFC = 4
    NG = FC // GFC
    GW = GFC * 128
    halves = [(hh, nn, min(512, D - nn))
              for hh, nn in enumerate(range(0, D, 512))]

    nc = bass.Bass("TRN2", target_bir_lowering=False)

    x_d = nc.dram_tensor("x", [T, D], F32, kind="ExternalInput").ap()
    out_d = nc.dram_tensor("out", [T, D], F32, kind="ExternalOutput").ap()
    w1_d = nc.dram_tensor("w1", [D, D], F32, kind="ExternalInput").ap()
    b1_d = nc.dram_tensor("b1", [1, D], F32, kind="ExternalInput").ap()
    w23_d = nc.dram_tensor("w23", [D, 32], F32, kind="ExternalInput").ap()
    b23_d = nc.dram_tensor("b23", [1, 32], F32, kind="ExternalInput").ap()
    wr1_d = nc.dram_tensor("wr1", [1, D], F32, kind="ExternalInput").ap()
    w1f_d = nc.dram_tensor("w1f", [D, F], F32, kind="ExternalInput").ap()
    b1fp_d = nc.dram_tensor("b1fp", [128, FC], F32, kind="ExternalInput").ap()
    w2f_d = nc.dram_tensor("w2f", [F, D], F32, kind="ExternalInput").ap()
    b2f_d = nc.dram_tensor("b2f", [1, D], F32, kind="ExternalInput").ap()
    s_d = nc.dram_tensor("s_sc", [T, D], BF16, kind="Internal").ap()
    st3_d = nc.dram_tensor("st3", [T, 3], F32, kind="Internal").ap()
    imap_d = nc.dram_tensor("imap", [CT, 1], I32, kind="Internal").ap()
    if cfg.get("DBG"):
        dbg_d = nc.dram_tensor("dbg", [128, 6, NT], F32,
                               kind="ExternalOutput").ap()

    with ExitStack() as ctx:
        e = ctx.enter_context
        w1_b = e(nc.sbuf_tensor("w1_b", [128, KC, D], BF16))
        w23_b = e(nc.sbuf_tensor("w23_b", [128, KC, 32], BF16))
        w1f_b = e(nc.sbuf_tensor("w1f_b", [128, KC, F], BF16))
        w2f_b = e(nc.sbuf_tensor("w2f_b", [128, FC, D], BF16))
        b1_b = e(nc.sbuf_tensor("b1_b", [1, D], BF16))
        b23_b = e(nc.sbuf_tensor("b23_b", [1, 32], BF16))
        b1f_p = e(nc.sbuf_tensor("b1f_p", [128, FC], F32))
        b1fc = e(nc.sbuf_tensor("b1fc", [128, FC], F32))
        b2f_b = e(nc.sbuf_tensor("b2f_b", [1, D], BF16))

        ident_b = e(nc.sbuf_tensor("ident_b", [128, 128], BF16))
        ltinc_b = e(nc.sbuf_tensor("ltinc_b", [128, 128], BF16))
        ones_b = e(nc.sbuf_tensor("ones_b", [1, 128], BF16))
        twos_b = e(nc.sbuf_tensor("twos_b", [1, 128], BF16))
        onesf_r = e(nc.sbuf_tensor("onesf_r", [1, 128], F32))
        ones_cb = e(nc.sbuf_tensor("ones_cb", [128, 1], BF16))
        ones_cf = e(nc.sbuf_tensor("ones_cf", [128, 1], F32))
        CONSTS = [LN_EPS, -0.5, -1.0, 1.0, 0.5, GELU_TANH_SCALE,
                  GELU_C2, 2.0, 0.25, 0.0, GELU_C2 ** 0.5]
        cpool = e(nc.sbuf_tensor("cpool", [128, len(CONSTS)], F32))
        for ci, cv in enumerate(CONSTS):
            nc.const_aps.register(F32, cv, cpool[:, ci:ci + 1])
        stat2 = e(nc.sbuf_tensor("stat2", [128, NT, 2], F32))
        rstd_a = e(nc.sbuf_tensor("rstd_a", [128, NT], F32))
        scores = e(nc.sbuf_tensor("scores", [128, NT], F32))
        wpre = e(nc.sbuf_tensor("wpre", [128, NT], F32))
        st3s = e(nc.sbuf_tensor("st3s", [128, NT, 3], F32))
        tid_i = e(nc.sbuf_tensor("tid_i", [128, NT], I32))
        biga = e(nc.sbuf_tensor("biga", [128, NT], F32))
        rowb = e(nc.sbuf_tensor("rowb", [128, NT], F32))
        rank_i = e(nc.sbuf_tensor("rank_i", [128, NT], I32))
        imap_s = e(nc.sbuf_tensor("imap_s", [128, CNT], I32))
        statc3 = e(nc.sbuf_tensor("statc3", [128, 2, 3], F32))
        big2 = rank_i[:, 0:CNT]
        if NT >= 32:
            prod_t = wpre[:, 0:32].rearrange("p (a b) -> p a b", a=2)
        else:
            prod_t = e(nc.sbuf_tensor("prod_t", [128, 2, 16], F32))
        x_slab = e(nc.sbuf_tensor("x_slab", [128, 2, D], F32))
        xn_slab = e(nc.sbuf_tensor("xn_slab", [128, 2, D], F32))
        s_slab = e(nc.sbuf_tensor("s_slab", [128, 2, D], F32))
        xnb_c = e(nc.sbuf_tensor("xnb_c", [128, D], BF16))
        xnT_b = e(nc.sbuf_tensor("xnT_b", [128, 2, KC, 128], BF16))
        A_t = e(nc.sbuf_tensor("A_t", [128, 2, D], F32))
        B_t = e(nc.sbuf_tensor("B_t", [128, 2, D], F32))
        A0 = A_t[:, 0, :]
        B0 = B_t[:, 0, :]
        wr_row = A0[0:1, 0:D]
        V_t = e(nc.sbuf_tensor("V_t", [128, D], BF16))
        bn6 = e(nc.sbuf_tensor("bn6", [128, 12], F32))
        ln_t = e(nc.sbuf_tensor("ln_t", [128, 2], F32))
        bc_s = e(nc.sbuf_tensor("bc_s", [128, 2, 32], F32))
        cb_t = e(nc.sbuf_tensor("cb_t", [128, 2], F32))
        hT_g = [e(nc.sbuf_tensor(f"hT_g{j}", [128, GFC, 128], BF16))
                for j in range(3)]
        maskbf = xnb_c[:, 0:NT]
        mask_a = B0[:, 0:NT]
        rank_f = B0[:, NT:2 * NT]
        tmp_r = B0[:, 2 * NT:3 * NT]
        tmp_r2 = B0[:, 3 * NT:4 * NT]
        th_t = B0[:, 4 * NT:4 * NT + 2 * NT]
        lo_t = A0[0:1, 0:ROWS]
        hi_t = A0[0:1, ROWS:2 * ROWS]
        m_t = A0[0:1, 2 * ROWS:3 * ROWS]
        d_t = A0[0:1, 3 * ROWS:4 * ROWS]
        ge_t = A0[0:1, 4 * ROWS:5 * ROWS]
        mb_s = A0[:, 16:16 + ROWS]
        pcnt = A0[:, 24:24 + ROWS]
        tot_r = A0[0:1, 512:512 + NT]
        offsA = A0[0:1, 576:576 + NT]
        offsB = A0[0:1, 640:640 + NT]
        X_g = [A_t[:, 0, 0:GW], A_t[:, 1, 0:GW]]
        Y_g = [B_t[:, 0, 0:GW], B_t[:, 1, 0:GW]]
        V_g = [V_t[:, 0:GW], V_t[:, GW:2 * GW]]
        psA = e(nc.psum_tensor("psA", [128, D], F32))
        psB = e(nc.psum_tensor("psB", [128, D], F32))
        psC = e(nc.psum_tensor("psC", [128, D], F32))
        psD = e(nc.psum_tensor("psD", [128, D], BF16))
        psE = e(nc.psum_tensor("psE", [128, 512], F32))
        sem_names = ["x0", "x1", "ws", "wm", "wb", "pe", "dv", "ac", "gq",
                     "st0", "st1", "sd0", "sd1", "ga", "gb", "im", "sst",
                     "sc0", "sc1", "ix", "sx0", "sx1", "h0", "h1",
                     "gs0", "gs1"]
        SEM = {n: e(nc.semaphore("s_" + n)) for n in sem_names}
        block = e(nc.Block())

        ops = {k: [] for k in ("sync", "gpsimd", "tensor", "vector", "scalar")}
        ctr = {n: 0 for n in sem_names}
        ENG_SEM = {"vector": ("dv", SEM["dv"]), "scalar": ("ac", SEM["ac"]),
                   "tensor": ("pe", SEM["pe"]), "gpsimd": ("gq", SEM["gq"])}
        M = {}

        def em(eng, fn):
            ops[eng].append(fn)

        def inc(name, n=1):
            ctr[name] += n
            return ctr[name]

        def wait(eng, name, val):
            if val > 0:
                em(eng, lambda E, s=SEM[name], v=val: E.wait_ge(s, v))

        def waitd(eng, name, key):
            em(eng, lambda E, s=SEM[name], k=key: E.wait_ge(s, M[k]))

        def op(eng, fn, tag=None):
            nm, sem = ENG_SEM[eng]
            em(eng, lambda E, f=fn, S=sem: f(E).then_inc(S, 1))
            v = inc(nm)
            if tag is not None:
                M[tag] = v
            return v

        def opq(eng, fn):
            em(eng, fn)

        def dma(eng, sem_name, fn):
            em(eng, lambda E, f=fn, S=SEM[sem_name]: f(E).then_inc(S, 16))
            return inc(sem_name, 16)

        REGS = {}

        def g_setup(E):
            rt = E.alloc_register()
            E.reg_mov(rt, T - 1)
            REGS["T"] = rt
            rc = E.alloc_register()
            E.reg_mov(rc, CT - 1)
            REGS["CT"] = rc
            E.memset(ident_b[:], 1.0)
            E.drain()
            E.affine_select(out=ident_b[:], in_=ident_b[:], pattern=[[-1, 128]],
                            compare_op=ALU.is_equal, fill=0.0,
                            base=0, channel_multiplier=1)
            E.memset(ltinc_b[:], 1.0)
            E.drain()
            E.affine_select(out=ltinc_b[:], in_=ltinc_b[:], pattern=[[1, 128]],
                            compare_op=ALU.is_gt, fill=0.0,
                            base=1, channel_multiplier=-1)
            E.memset(ones_b[:], 1.0)
            E.memset(twos_b[:], 2.0)
            E.memset(onesf_r[:], 1.0)
            E.memset(ones_cb[:], 1.0)
            E.memset(ones_cf[:], 1.0)
            for ci, cv in enumerate(CONSTS):
                E.memset(cpool[:, ci:ci + 1], cv)
            E.iota(tid_i[:], pattern=[[128, NT]], base=0, channel_multiplier=1)
            E.iota(biga[:], pattern=[[128, NT]], base=int(BIGV),
                   channel_multiplier=1, allow_small_or_imprecise_dtypes=True)
            E.iota(rowb[:], pattern=[[KTGT, ROWS], [0, RC]], base=0,
                   channel_multiplier=0, allow_small_or_imprecise_dtypes=True)
            E.iota(big2[:], pattern=[[128, CNT]], base=2 * int(BIGV),
                   channel_multiplier=1)
            return E.drain()
        op("gpsimd", g_setup, "setup")
        wait("scalar", "gq", M["setup"])
        wait("vector", "gq", M["setup"])
        wait("tensor", "gq", M["setup"])

        dma("gpsimd", "im", lambda E: E.dma_start(
            out=imap_d.rearrange("(a b) o -> a (b o)", a=128), in_=big2[:]))
        IM_PREFILL = ctr["im"]

        small_loads = [
            (wr_row[:], wr1_d[:]),
            (w23_b[:], w23_d.rearrange("(c p) n -> p c n", p=128)),
            (b1_b[:], b1_d[:]),
            (b23_b[:], b23_d[:]),
            (b1f_p[:], b1fp_d[:]),
            (b2f_b[:], b2f_d[:]),
        ]
        for dst, src in small_loads:
            dma("gpsimd", "wm", lambda E, d=dst, s=src: E.dma_start(
                out=d, in_=s))
        W_SMALL = ctr["wm"]
        dma("gpsimd", "wb", lambda E: E.dma_start(
            out=w1_b[:], in_=w1_d.rearrange("(c p) n -> p c n", p=128)))
        dma("gpsimd", "wb", lambda E: E.dma_start(
            out=w1f_b[:], in_=w1f_d.rearrange("(c p) n -> p c n", p=128)))
        W_P12 = ctr["wb"]

        wait("vector", "wm", W_SMALL)
        op("vector", lambda E: E.tensor_scalar_mul(
            b1fc[:], b1f_p[:], GELU_C2 ** 0.5), "b1fc")

        wait("tensor", "wm", W_SMALL)
        opq("tensor", lambda E: E.matmul(psB[:, 0:512], onesf_r[0:1, :],
                                         wr_row[0:1, 0:512],
                                         start=True, stop=True))
        op("tensor", lambda E: E.matmul(psB[:, 512:1024], onesf_r[0:1, :],
                                        wr_row[0:1, 512:1024],
                                        start=True, stop=True), "wrbc")

        wait("tensor", "wb", W_P12)

        def p12_s1(i):
            par = i % 2
            xpar = "x%d" % par
            xc = x_slab[:, par, :]
            if i >= 2:
                wait("sync", "sx%d" % par, 16 * ((i - 2) // 2 + 1))
            dma("sync", "h%d" % par, lambda E, ii=i, b=xc: E.dma_start(
                out=b[:, 0:512], in_=x_d[128 * ii:128 * (ii + 1), 0:512]))
            xh = ctr["h%d" % par]
            dma("sync", xpar, lambda E, ii=i, b=xc: E.dma_start(
                out=b[:, 512:1024], in_=x_d[128 * ii:128 * (ii + 1),
                                            512:1024]))
            xv = ctr[xpar]
            wait("vector", "h%d" % par, xh)
            if i >= 1:
                wait("vector", "dv", M[("bn", i - 1)])
            opq("vector", lambda E, b=xc: E.bn_stats(
                bn6[:, 0:6], b[:, 0:512]))
            wait("vector", xpar, xv)
            op("vector", lambda E, b=xc: E.bn_stats(
                bn6[:, 6:12], b[:, 512:1024]), ("bns", i))
            wait("vector", "dv", M[("bns", i)])
            op("vector", lambda E, ii=i: E.bn_aggr(stat2[:, ii, :], bn6[:]),
               ("bn", i))

        def p12_s2(i):
            par = i % 2
            xc = x_slab[:, par, :]
            xnc = xn_slab[:, par, :]
            Bp = B_t[:, par, :]
            psP = psA if par == 0 else psC
            bce0 = 0 if par == 0 else 256

            wait("vector", "ac", M[("rstd", i)])
            if i >= 2:
                wait("vector", "ac", M[("Ssil", i - 2)])
                wait("vector", "dv", M[("s8", i - 2)])
            op("vector", lambda E, b=xc, o=xnc, ii=i: E.tensor_scalar(
                out=o, in0=b, scalar1=stat2[:, ii, 0:1],
                scalar2=rstd_a[:, ii:ii + 1], op0=ALU.subtract, op1=ALU.mult),
               ("xn", i))
            wait("sync", "dv", M[("xn", i)])
            dma("sync", "sx%d" % par, lambda E, ii=i, b=xc: E.dma_start(
                out=out_d[128 * ii:128 * (ii + 1), :], in_=b))

            wait("vector", "dv", M[("xn", i)])
            if i >= 1:
                wait("vector", "pe", M[("btr", i - 1)])
            op("vector", lambda E, o=xnc: E.tensor_copy(xnb_c[:], o),
               ("cast", i))
            wait("tensor", "dv", M[("cast", i)])
            for k in range(KC):
                fn = (lambda E, kk=k: E.transpose(
                    psD[:, 128 * kk:128 * (kk + 1)],
                    xnb_c[:, 128 * kk:128 * (kk + 1)], ident_b[:]))
                if k == KC - 1:
                    op("tensor", fn, ("btr", i))
                else:
                    opq("tensor", lambda E, f=fn: f(E))
            if i == 0:
                wait("vector", "pe", M["wrbc"])
            op("vector", lambda E, o=xnc, ii=i: E.scalar_tensor_tensor(
                out=V_t[:], in0=o, scalar=1.0, in1=psB[:, 0:D],
                op0=ALU.mult, op1=ALU.mult,
                accum_out=scores[:, ii:ii + 1]), ("scr", i))
            wait("vector", "pe", M[("btr", i)])
            op("vector", lambda E, ii=i: E.tensor_copy(
                xnT_b[:, ii % 2, :, :],
                psD[:, 0:D].rearrange("p (k t) -> p k t", k=KC)),
               ("tev", i))

            wait("tensor", "dv", M[("tev", i)])
            if i >= 2:
                wait("tensor", "ac", M[("spl", i - 2)])
                wait("tensor", "dv", M[("s8", i - 2)])
            for h, n0, w in halves:
                opq("tensor", lambda E, nn=n0, ww=w, P=psP: E.matmul(
                    P[:, nn:nn + ww], ones_b[:], b1_b[0:1, nn:nn + ww],
                    start=True, stop=False))
                for k in range(KC):
                    fn = (lambda E, ii=i, kk=k, nn=n0, ww=w, P=psP:
                          E.matmul(P[:, nn:nn + ww], xnT_b[:, ii % 2, kk, :],
                                   w1_b[:, kk, nn:nn + ww],
                                   start=False, stop=(kk == KC - 1)))
                    if h == len(halves) - 1 and k == KC - 1:
                        op("tensor", fn, ("dmm", i))
                    else:
                        opq("tensor", lambda E, f=fn: f(E))
            if i >= 2:
                wait("tensor", "dv", M[("bce", i - 2)])
            opq("tensor", lambda E, b0=bce0: E.matmul(
                psE[:, b0:b0 + 32], ones_b[:], b23_b[:],
                start=True, stop=False))
            for k in range(KC):
                fn = (lambda E, ii=i, kk=k, b0=bce0: E.matmul(
                    psE[:, b0:b0 + 32], xnT_b[:, ii % 2, kk, :],
                    w23_b[:, kk, :], start=False, stop=(kk == KC - 1)))
                if k == KC - 1:
                    op("tensor", fn, ("bcm", i))
                else:
                    opq("tensor", lambda E, f=fn: f(E))

        def p12_s3(i):
            par = i % 2
            xc = x_slab[:, par, :]
            xnc = xn_slab[:, par, :]
            sc_ = s_slab[:, par, :]
            Ap = A_t[:, par, :]
            Bp = B_t[:, par, :]
            psP = psA if par == 0 else psC
            bce0 = 0 if par == 0 else 256

            wait("scalar", "pe", M[("dmm", i)])
            if i >= 2:
                wait("scalar", "dv", M[("q3", i - 2)])
            op("scalar", lambda E, P=psP, Bb=Bp: E.activation(
                out=Bb, in_=P[:, 0:D], func=AF.Exp), ("spe", i))
            wait("scalar", "ac", M[("spe", i)])
            if i >= 2:
                wait("scalar", "st%d" % par,
                     16 * ((i - 2) // 2 + 1))
            op("scalar", lambda E, Aa=Ap, Bb=Bp: E.activation(
                out=Aa, in_=Bb, func=AF.Ln, bias=1.0), ("spl", i))
            M[("dstt", i)] = M[("spl", i)]
            wait("scalar", "ac", M[("spl", i)])
            wait("scalar", "dv", M[("scr", i)])
            op("scalar", lambda E, o=xnc, Bb=Bp: E.activation(
                out=Bb, in_=o, func=AF.Silu), ("Ssil", i))
            wait("vector", "pe", M[("bcm", i)])
            op("vector", lambda E, ii=i, b0=bce0: E.tensor_copy(
                bc_s[:, ii % 2, :], psE[:, b0:b0 + 32]), ("bce", i))
            wait("vector", "dv", M[("bce", i)])
            op("vector", lambda E, ii=i: E.tensor_mul(
                prod_t[:, ii % 2, :], bc_s[:, ii % 2, 0:16],
                bc_s[:, ii % 2, 16:32]), None)
            wait("vector", "dv", ctr["dv"])
            op("vector", lambda E, ii=i: E.reduce_sum(
                cb_t[:, ii % 2:ii % 2 + 1], prod_t[:, ii % 2, :],
                axis=mybir.AxisListType.X), ("cb", i))
            wait("vector", "dv", M[("cb", i)])
            wait("vector", "ac", M[("spl", i)])
            if i >= 2:
                wait("vector", "sd%d" % par, 16 * ((i - 2) // 2 + 1))
                wait("vector", "dv", M[("q3", i - 2)])
            op("vector", lambda E, o=xnc, s2=sc_, ii=i, Aa=Ap:
               E.scalar_tensor_tensor(
                   out=s2, in0=o, scalar=cb_t[:, ii % 2:ii % 2 + 1], in1=Aa,
                   op0=ALU.mult, op1=ALU.mult), ("s8", i))
            wait("gpsimd", "dv", M[("s8", i)])
            dma("gpsimd", "sd%d" % par, lambda E, ii=i, s2=sc_: E.dma_start(
                out=s_d[128 * ii:128 * (ii + 1), :], in_=s2))

            wait("vector", "ac", M[("Ssil", i)])
            wait("vector", "dv", M[("s8", i)])
            op("vector", lambda E, s2=sc_, Aa=Ap, Bb=Bp: E.tensor_mul(
                Aa, s2, Bb), ("q3", i))
            wait("gpsimd", "sx%d" % par, 16 * (i // 2 + 1))
            wait("gpsimd", "dv", M[("q3", i)])
            dma("gpsimd", "st%d" % par, lambda E, ii=i, Aa=Ap: E.dma_start(
                out=out_d[128 * ii:128 * (ii + 1), :], in_=Aa,
                accum_op=ALU.add))
            i2 = i + 2
            if i2 < NT:
                wait("scalar", "dv", M[("bn", i2)])
                op("scalar", lambda E, ii=i2: E.activation(
                    out=ln_t[:, ii % 2:ii % 2 + 1], in_=stat2[:, ii, 1:2],
                    func=AF.Ln, bias=LN_EPS), ("lnv", i2))
                wait("scalar", "ac", M[("lnv", i2)])
                op("scalar", lambda E, ii=i2: E.activation(
                    out=rstd_a[:, ii:ii + 1],
                    in_=ln_t[:, ii % 2:ii % 2 + 1],
                    func=AF.Exp, scale=-0.5), ("rstd", i2))

        p12_s1(0)
        p12_s1(1)
        for i0 in (0, 1):
            wait("scalar", "dv", M[("bn", i0)])
            op("scalar", lambda E, ii=i0: E.activation(
                out=ln_t[:, ii % 2:ii % 2 + 1], in_=stat2[:, ii, 1:2],
                func=AF.Ln, bias=LN_EPS), ("lnv", i0))
            wait("scalar", "ac", M[("lnv", i0)])
            op("scalar", lambda E, ii=i0: E.activation(
                out=rstd_a[:, ii:ii + 1], in_=ln_t[:, ii % 2:ii % 2 + 1],
                func=AF.Exp, scale=-0.5), ("rstd", i0))
        p12_s2(0)
        for i in range(NT):
            if i + 2 < NT:
                p12_s1(i + 2)
            if i + 1 < NT:
                p12_s2(i + 1)
            p12_s3(i)

        ST_TOT = {0: ctr["st0"], 1: ctr["st1"]}
        SD_TOT = (ctr["sd0"], ctr["sd1"])

        dma("gpsimd", "wb", lambda E: E.dma_start(
            out=w2f_b[:], in_=w2f_d.rearrange("(c p) n -> p c n", p=128)))
        W_ALL = ctr["wb"]

        wait("scalar", "dv", M[("scr", NT - 1)])
        wait("scalar", "dv", M[("cb", NT - 1)])
        op("scalar", lambda E: E.activation(
            out=wpre[:], in_=scores[:], func=AF.Exp, scale=-1.0), "wpe")
        wait("scalar", "ac", M["wpe"])
        op("scalar", lambda E: E.activation(
            out=wpre[:], in_=wpre[:], func=AF.Identity, bias=1.0), "wp1")
        wait("vector", "ac", M["wp1"])
        op("vector", lambda E: E.reciprocal(wpre[:], wpre[:]), "wpr")
        wait("vector", "dv", M["wpr"])
        op("vector", lambda E: E.tensor_scalar_mul(wpre[:], wpre[:], 0.5),
           "wps")

        wait("vector", "dv", M["wps"])
        opq("vector", lambda E: E.tensor_copy(st3s[:, :, 0],
                                              stat2[:, :, 0]))
        opq("vector", lambda E: E.tensor_copy(st3s[:, :, 1], rstd_a[:]))
        op("vector", lambda E: E.tensor_copy(st3s[:, :, 2], wpre[:]),
           "st3c")
        wait("sync", "dv", M["st3c"])
        NT2 = NT // 2
        for hh in range(2):
            dma("sync", "sst", lambda E, h2=hh: E.dma_start(
                out=st3_d[128 * NT2 * h2:128 * NT2 * (h2 + 1), :].rearrange(
                    "(i p) k -> p i k", p=128),
                in_=st3s[:, NT2 * h2:NT2 * (h2 + 1), :],
                allow_slow_non_contiguous=True))
        SST_TOT = ctr["sst"]

        wait("vector", "dv", ctr["dv"])
        wait("vector", "ac", M[("Ssil", NT - 1)])
        wait("vector", "st0", ST_TOT[0])
        wait("vector", "st1", ST_TOT[1])
        op("vector", lambda E: E.memset(lo_t, -LOHI), None)
        op("vector", lambda E: E.memset(hi_t, LOHI), "binit")

        for j in range(BI):
            wait("vector", "dv", ctr["dv"])
            op("vector", lambda E: E.tensor_add(m_t, lo_t, hi_t), None)
            wait("vector", "dv", ctr["dv"])
            op("vector", lambda E: E.tensor_scalar_mul(m_t, m_t, 0.5),
               ("bm", j))
            wait("tensor", "dv", M[("bm", j)])
            op("tensor", lambda E: E.matmul(psE[:, 0:ROWS], onesf_r[:], m_t,
                                            start=True, stop=True), ("bmb", j))
            wait("vector", "pe", M[("bmb", j)])
            for r in range(ROWS):
                fn = (lambda E, rr=r: E.tensor_scalar(
                    out=th_t[:, rr * RC:(rr + 1) * RC],
                    in0=scores[:, rr * RC:(rr + 1) * RC],
                    scalar1=psE[:, rr:rr + 1], scalar2=0.0,
                    op0=ALU.is_gt, op1=ALU.add,
                    accum_out=pcnt[:, rr:rr + 1]))
                if r == ROWS - 1:
                    op("vector", fn, ("bcmp", j))
                else:
                    opq("vector", lambda E, f=fn: f(E))
            wait("tensor", "dv", M[("bcmp", j)])
            op("tensor", lambda E: E.matmul(psE[0:1, 4:4 + ROWS], ones_cf[:],
                                            pcnt, start=True, stop=True),
               ("bcnt", j))
            wait("vector", "pe", M[("bcnt", j)])
            op("vector", lambda E: E.tensor_scalar(
                out=ge_t, in0=psE[0:1, 4:4 + ROWS], scalar1=float(KTGT),
                scalar2=None, op0=ALU.is_ge), None)
            op("vector", lambda E: E.tensor_sub(d_t, m_t, lo_t), None)
            wait("vector", "dv", ctr["dv"])
            op("vector", lambda E: E.tensor_mul(d_t, d_t, ge_t), None)
            wait("vector", "dv", ctr["dv"])
            op("vector", lambda E: E.tensor_add(lo_t, lo_t, d_t), None)
            wait("vector", "dv", ctr["dv"])
            op("vector", lambda E: E.tensor_sub(d_t, hi_t, m_t), None)
            wait("vector", "dv", ctr["dv"])
            op("vector", lambda E: E.tensor_mul(d_t, d_t, ge_t), None)
            wait("vector", "dv", ctr["dv"])
            op("vector", lambda E: E.tensor_add(hi_t, m_t, d_t), ("bupd", j))

        wait("tensor", "dv", M[("bupd", BI - 1)])
        op("tensor", lambda E: E.matmul(psE[:, 0:ROWS], onesf_r[:], lo_t,
                                        start=True, stop=True), "bmaskmb")
        wait("vector", "pe", M["bmaskmb"])
        op("vector", lambda E: E.tensor_copy(mb_s, psE[:, 0:ROWS]), "bmbc2")
        wait("vector", "dv", M["bmbc2"])
        for r in range(ROWS):
            fn = (lambda E, rr=r: E.tensor_scalar(
                out=mask_a[:, rr * RC:(rr + 1) * RC],
                in0=scores[:, rr * RC:(rr + 1) * RC],
                scalar1=mb_s[:, rr:rr + 1], scalar2=None, op0=ALU.is_gt))
            if r == ROWS - 1:
                op("vector", fn, "bmask")
            else:
                opq("vector", lambda E, f=fn: f(E))

        wait("vector", "dv", M["bmask"])
        wait("vector", "pe", M[("btr", NT - 1)])
        op("vector", lambda E: E.tensor_copy(maskbf, mask_a), "mkb")
        wait("tensor", "dv", M["mkb"])
        op("tensor", lambda E: E.matmul(psE[:, 256:256 + NT], ltinc_b[:],
                                        maskbf, start=True, stop=True),
           "incl")
        op("tensor", lambda E: E.matmul(psE[0:1, 320:320 + NT], ones_cb[:],
                                        maskbf, start=True, stop=True), "totm")
        wait("vector", "pe", M["totm"])
        op("vector", lambda E: E.tensor_copy(rank_f, psE[:, 256:256 + NT]),
           "rk0")
        op("vector", lambda E: E.tensor_copy(tot_r, psE[0:1, 320:320 + NT]),
           "totc")
        wait("vector", "dv", M["totc"])
        for r in range(ROWS):
            a = offsA[0:1, r * RC:(r + 1) * RC]
            b = offsB[0:1, r * RC:(r + 1) * RC]
            t = tot_r[0:1, r * RC:(r + 1) * RC]
            opq("vector", lambda E, aa=a: E.memset(aa[0:1, 0:1], 0.0))
            op("vector", lambda E, aa=a, tt=t: E.tensor_copy(
                aa[0:1, 1:RC], tt[0:1, 0:RC - 1]), ("cse", r, 0))
            src, dst = a, b
            kk_ = 1
            step = 0
            while kk_ < RC:
                wait("vector", "dv", M[("cse", r, step)])
                opq("vector", lambda E, s3=src, d3=dst, k4=kk_: E.tensor_copy(
                    d3[0:1, 0:k4], s3[0:1, 0:k4]))
                op("vector", lambda E, s3=src, d3=dst, k4=kk_: E.tensor_add(
                    d3[0:1, k4:RC], s3[0:1, k4:RC], s3[0:1, 0:RC - k4]),
                   ("cse", r, step + 1))
                src, dst = dst, src
                kk_ *= 2
                step += 1
            if src is not offsA:
                wait("vector", "dv", M[("cse", r, step)])
                op("vector", lambda E, s3=src, aa=a: E.tensor_copy(aa, s3),
                   ("cse", r, step + 1))
            M[("csef", r)] = ctr["dv"]
        wait("tensor", "dv", max(M[("csef", r)] for r in range(ROWS)))
        op("tensor", lambda E: E.matmul(psE[:, 384:384 + NT], onesf_r[:],
                                        offsA[0:1, 0:NT], start=True,
                                        stop=True), "offb")
        wait("vector", "pe", M["offb"])
        op("vector", lambda E: E.tensor_sub(rank_f, rank_f, mask_a), None)
        wait("vector", "dv", ctr["dv"])
        op("vector", lambda E: E.tensor_add(rank_f, rank_f,
                                            psE[:, 384:384 + NT]), "rk1")
        wait("vector", "dv", M["rk1"])
        op("vector", lambda E: E.tensor_scalar(
            out=tmp_r, in0=rank_f, scalar1=float(KTGT), scalar2=None,
            op0=ALU.is_ge), None)
        op("vector", lambda E: E.tensor_scalar(
            out=tmp_r2, in0=mask_a, scalar1=-1.0, scalar2=1.0,
            op0=ALU.mult, op1=ALU.add), None)
        wait("vector", "dv", ctr["dv"])
        op("vector", lambda E: E.tensor_add(tmp_r, tmp_r, tmp_r2), None)
        wait("vector", "dv", ctr["dv"])
        op("vector", lambda E: E.tensor_mul(tmp_r, tmp_r, biga[:]), None)
        wait("vector", "dv", ctr["dv"])
        op("vector", lambda E: E.tensor_add(rank_f, rank_f, tmp_r), None)
        wait("vector", "dv", ctr["dv"])
        op("vector", lambda E: E.tensor_add(rank_f, rank_f, rowb[:]), "rk2")
        wait("vector", "dv", M["rk2"])
        wait("vector", "im", IM_PREFILL)
        op("vector", lambda E: E.tensor_copy(rank_i[:], rank_f), "rki")

        def emit_im_scatter(c):
            dma("gpsimd", "im", lambda E, cc=c: E.indirect_dma_start(
                out=imap_d, out_offset=IndirectOffsetOnAxis(
                    ap=rank_i[:, cc:cc + 1], axis=0),
                in_=tid_i[:, cc:cc + 1], in_offset=None,
                bounds_check=REGS["CT"], oob_is_err=False))
        wait("gpsimd", "dv", M["rki"])
        wait("gpsimd", "im", IM_PREFILL)
        for c in range(RC):
            emit_im_scatter(c)
        IM_H1 = ctr["im"]
        wait("sync", "im", IM_H1)
        dma("sync", "ix", lambda E: E.dma_start(
            out=imap_s[:, 0:CNT // 2],
            in_=imap_d[0:CT // 2, :].rearrange("(c p) o -> p (c o)", p=128),
            allow_slow_non_contiguous=True))
        IMS_V = ctr["ix"]

        wait("tensor", "wb", W_ALL)
        wait("gpsimd", "ix", IMS_V)
        wait("gpsimd", "sd0", SD_TOT[0])
        wait("gpsimd", "sd1", SD_TOT[1])
        wait("gpsimd", "sst", SST_TOT)

        def emit_gathers(j):
            c = j % 2
            gpar = "ga" if c == 0 else "gb"
            if j >= CNT // 2:
                wait("gpsimd", "ix", 32)
            if j >= 2:
                wait("gpsimd", "dv", M[("Vo", j - 2)])
                wait("gpsimd", "sc%d" % (j % 2),
                     16 * ((j - 2) // 2 + 1))
            dma("gpsimd", gpar, lambda E, jj=j, cc=c: E.indirect_dma_start(
                out=statc3[:, cc, :], out_offset=None, in_=st3_d,
                in_offset=IndirectOffsetOnAxis(
                    ap=imap_s[:, jj:jj + 1], axis=0),
                bounds_check=REGS["T"], oob_is_err=False))
            dma("gpsimd", gpar, lambda E, jj=j, cc=c: E.indirect_dma_start(
                out=x_slab[:, cc, :], out_offset=None, in_=x_d,
                in_offset=IndirectOffsetOnAxis(
                    ap=imap_s[:, jj:jj + 1], axis=0),
                bounds_check=REGS["T"], oob_is_err=False))
            M[("gat", j)] = ctr[gpar]
            dma("gpsimd", "gs%d" % c, lambda E, jj=j, cc=c:
                E.indirect_dma_start(
                    out=s_slab[:, cc, :], out_offset=None, in_=s_d,
                    in_offset=IndirectOffsetOnAxis(
                        ap=imap_s[:, jj:jj + 1], axis=0),
                    bounds_check=REGS["T"], oob_is_err=False))
            M[("gats", j)] = ctr["gs%d" % c]

        PB2 = [psB[:, 0:512], psB[:, 512:1024], psE[:, 0:512]]
        VFE = {}

        def emit_head_v(j):
            c = j % 2
            gpar = "ga" if c == 0 else "gb"
            xc = x_slab[:, c, :]
            xnc = xn_slab[:, c, :]

            wait("vector", gpar, M[("gat", j)])
            if j < 2:
                wait("vector", "st%d" % (j % 2), ST_TOT[j % 2])
            else:
                wait("vector", "dv", M[("Vo", j - 2)])
            op("vector", lambda E, b=xc, o=xnc, cc=c: E.tensor_scalar(
                out=o, in0=b, scalar1=statc3[:, cc, 0:1],
                scalar2=statc3[:, cc, 1:2], op0=ALU.subtract, op1=ALU.mult),
               ("xnb2", j))
            wait("vector", "dv", M[("xnb2", j)])
            if j >= 1:
                wait("vector", "pe", M[("btr2", j - 1)])
            op("vector", lambda E, o=xnc: E.tensor_copy(xnb_c[:], o),
               ("cst2", j))

        def emit_head_pe(j):
            wait("tensor", "dv", M[("cst2", j)])
            for k in range(KC):
                fn = (lambda E, kk=k: E.transpose(
                    psD[:, 128 * kk:128 * (kk + 1)],
                    xnb_c[:, 128 * kk:128 * (kk + 1)], ident_b[:]))
                if k == KC - 1:
                    op("tensor", fn, ("btr2", j))
                else:
                    opq("tensor", lambda E, f=fn: f(E))
            wait("vector", "pe", M[("btr2", j)])
            op("vector", lambda E, jj=j: E.tensor_copy(
                xnT_b[:, jj % 2, :, :],
                psD[:, 0:D].rearrange("p (k t) -> p k t", k=KC)),
               ("tev2", j))

        def emit_body(j):
            c = j % 2
            xc = x_slab[:, c, :]
            xnc = xn_slab[:, c, :]
            sc_ = s_slab[:, c, :]
            psM = psC if c == 0 else psA

            def emit_m1(g):
                G = NG * j + g
                pb = PB2[G % 3].rearrange("p (j t) -> p j t", j=GFC)
                if g == 0:
                    wait("tensor", "dv", M[("tev2", j)])
                if G >= 3:
                    pg = divmod(G - 3, NG)
                    wait("tensor", VFE[pg], M[("vfree",) + pg])
                for jj in range(GFC):
                    fc = GFC * g + jj
                    for k in range(KC):
                        fn = (lambda E, f=fc, j2=jj, kk=k, P=pb, jc=j:
                              E.matmul(P[:, j2, :],
                                       w1f_b[:, kk, 128 * f:128 * (f + 1)],
                                       xnT_b[:, jc % 2, kk, :],
                                       start=(kk == 0), stop=(kk == KC - 1)))
                        if jj == GFC - 1 and k == KC - 1:
                            op("tensor", fn, ("m1", j, g))
                        else:
                            opq("tensor", lambda E, f=fn: f(E))

            def emit_evac(g):
                G = NG * j + g
                b = G % 2
                bk = G % 3
                eng = "vector"
                VFE[(j, g)] = "dv"
                wait(eng, "pe", M[("m1", j, g)])
                if G >= 2:
                    pg = divmod(G - 2, NG)
                    wait(eng, "ac", M[("sq",) + pg])
                    wait(eng, "dv", M[("h",) + pg])
                for jj in range(GFC):
                    fc = GFC * g + jj
                    if eng == "vector":
                        fn = (lambda E, bb=b, bkk=bk, j2=jj, f=fc:
                              E.tensor_scalar(
                                  out=V_t[:, bb * GW + j2 * 128:
                                          bb * GW + (j2 + 1) * 128],
                                  in0=PB2[bkk][:, j2 * 128:(j2 + 1) * 128],
                                  scalar1=b1f_p[:, f:f + 1], scalar2=None,
                                  op0=ALU.add))
                    else:
                        fn = (lambda E, bb=b, bkk=bk, j2=jj, f=fc:
                              E.activation(
                                  out=V_t[:, bb * GW + j2 * 128:
                                          bb * GW + (j2 + 1) * 128],
                                  in_=PB2[bkk][:, j2 * 128:(j2 + 1) * 128],
                                  func=AF.Identity, bias=b1f_p[:, f:f + 1]))
                    if jj == GFC - 1:
                        op(eng, fn, ("vfree", j, g))
                    else:
                        opq(eng, lambda E, f=fn: f(E))

            def emit_sq(g):
                G = NG * j + g
                b = G % 2
                wait("scalar", VFE[(j, g)], M[("vfree", j, g)])
                if G >= 2:
                    pg = divmod(G - 2, NG)
                    wait("scalar", "dv", M[("h",) + pg])
                op("scalar", lambda E, bb=b: E.activation(
                    out=X_g[bb], in_=V_g[bb], func=AF.Square,
                    scale=GELU_C2 ** 0.5), ("sq", j, g))

            def emit_u(gj, g):
                G = NG * gj + g
                b = G % 2
                wait("vector", "ac", M[("sq", gj, g)])
                if G >= 2:
                    pg = divmod(G - 2, NG)
                    wait("vector", "ac", M[("tanh",) + pg])
                if G >= 1 and g == 0:
                    wait("vector", "dv", M[("Vo", gj - 1)])
                op("vector", lambda E, bb=b: E.scalar_tensor_tensor(
                    out=Y_g[bb], in0=X_g[bb], scalar=1.0, in1=V_g[bb],
                    op0=ALU.add, op1=ALU.mult), ("u", gj, g))

            def emit_tanh(gj, g):
                wait("scalar", "dv", M[("u", gj, g)])
                op("scalar", lambda E, bb=(NG * gj + g) % 2: E.activation(
                    out=X_g[bb], in_=Y_g[bb], func=AF.Tanh,
                    scale=GELU_TANH_SCALE), ("tanh", gj, g))

            def emit_h(gj, g):
                G = NG * gj + g
                b = G % 2
                wait("vector", "ac", M[("tanh", gj, g)])
                if G >= 3:
                    pg = divmod(G - 3, NG)
                    wait("vector", "pe", M[("m2p",) + pg])
                op("vector", lambda E, bb=b, b3=G % 3: E.scalar_tensor_tensor(
                    out=hT_g[b3].rearrange("p j t -> p (j t)"),
                    in0=X_g[bb], scalar=1.0, in1=V_g[bb],
                    op0=ALU.add, op1=ALU.mult), ("h", gj, g))

            def emit_pbias():
                wait("tensor", "dv", M[("tev2", j)])
                if j >= 2:
                    wait("tensor", "dv", M[("Vy", j - 2)])
                for h, n0, w in halves:
                    fn = (lambda E, nn=n0, ww=w, P=psM: E.matmul(
                        P[:, nn:nn + ww], twos_b[:], b2f_b[0:1, nn:nn + ww],
                        start=True, stop=False))
                    if h == len(halves) - 1:
                        op("tensor", fn, ("pbias", j))
                    else:
                        opq("tensor", lambda E, f=fn: f(E))

            def emit_m2p(gj, g):
                G = NG * gj + g
                b = G % 3
                wait("tensor", "dv", M[("h", gj, g)])
                if gj == j and g == 0:
                    wait("tensor", "pe", M[("pbias", j)])
                n_mm = len(halves) * GFC
                mi = 0
                for h, n0, w in halves:
                    for jj in range(GFC):
                        fc = GFC * g + jj
                        mi += 1
                        stop = (g == NG - 1 and jj == GFC - 1)
                        fn = (lambda E, f=fc, j2=jj, nn=n0, ww=w,
                              P=(psC if gj % 2 == 0 else psA),
                              bb=b, st=stop: E.matmul(
                                  P[:, nn:nn + ww], hT_g[bb][:, j2, :],
                                  w2f_b[:, f, nn:nn + ww],
                                  start=False, stop=st))
                        if mi == n_mm:
                            op("tensor", fn, ("m2p", gj, g))
                        else:
                            opq("tensor", lambda E, f=fn: f(E))

            def back(k, d):
                gg = k - d
                if gg >= NG:
                    return None
                if gg >= 0:
                    return (j, gg)
                if j == 0:
                    return None
                return (j - 1, NG + gg)

            for k in range(NG + 4):
                if k < NG:
                    emit_m1(k)
                if k == 5 and j + 1 < CNT:
                    emit_head_v(j + 1)
                if k == 6 and j + 1 < CNT:
                    emit_head_pe(j + 1)
                t1 = back(k, 1)
                if k == 1 and j + 1 < CNT:
                    emit_gathers(j + 1)
                if t1 and t1[0] == j:
                    emit_u(*t1)
                    emit_tanh(*t1)
                t2 = back(k, 2)
                if t2 and t2[0] == j:
                    emit_h(*t2)
                if k == 2:
                    emit_pbias()
                t4 = back(k, 4)
                if t4 and t4[0] == j:
                    emit_m2p(*t4)
                if k < NG:
                    emit_evac(k)
                    emit_sq(k)
            M[("m2d", j)] = M[("m2p", j, NG - 1)]

            wait("vector", "pe", M[("m2d", j)])
            op("vector", lambda E, o=xnc, P=psM, cc=c: E.scalar_tensor_tensor(
                out=o, in0=P[:, 0:D], scalar=statc3[:, cc, 2:3], in1=o,
                op0=ALU.mult, op1=ALU.add), ("Vy", j))
            wait("scalar", "dv", M[("Vy", j)])
            op("scalar", lambda E, o=xnc: E.activation(
                out=B0, in_=o, func=AF.Silu), ("Sy", j))
            wait("vector", "ac", M[("Sy", j)])
            wait("vector", "gs%d" % c, M[("gats", j)])
            op("vector", lambda E, o=xnc, s2=sc_: E.tensor_mul(
                o, B0, s2), ("Vt", j))
            wait("vector", "dv", M[("Vt", j)])
            op("vector", lambda E, o=xnc, b=xc: E.tensor_add(b, b, o),
               ("Vo", j))
            if j == 0:
                wait("gpsimd", "st0", ST_TOT[0])
                wait("gpsimd", "st1", ST_TOT[1])
            wait("gpsimd", "dv", M[("Vo", j)])
            dma("gpsimd", "sc%d" % c, lambda E, jj=j, b=xc:
                E.indirect_dma_start(
                    out=out_d, out_offset=IndirectOffsetOnAxis(
                        ap=imap_s[:, jj:jj + 1], axis=0),
                    in_=b, in_offset=None,
                    bounds_check=REGS["T"], oob_is_err=False))

        emit_gathers(0)
        emit_head_v(0)
        emit_head_pe(0)
        rest = list(range(RC, NT))
        nb = max(1, CNT // 2 - 2)
        per = max(1, (len(rest) + nb - 1) // nb)
        for j in range(CNT):
            emit_body(j)
            if rest:
                for c in rest[:per]:
                    emit_im_scatter(c)
                rest = rest[per:]
                if not rest:
                    wait("sync", "im", ctr["im"])
                    dma("sync", "ix", lambda E: E.dma_start(
                        out=imap_s[:, CNT // 2:CNT],
                        in_=imap_d[CT // 2:CT, :].rearrange(
                            "(c p) o -> p (c o)", p=128),
                        allow_slow_non_contiguous=True))

        if cfg.get("DBG"):
            wait("gpsimd", "dv", M["rki"])
            def g_dbg(E):
                E.dma_start(out=dbg_d[:, 0, :], in_=scores[:]).then_inc(
                    SEM["im"], 16)
                E.dma_start(out=dbg_d[:, 1, :], in_=mask_a).then_inc(
                    SEM["im"], 16)
                E.dma_start(out=dbg_d[:, 2, :], in_=rank_f).then_inc(
                    SEM["im"], 16)
                E.dma_start(out=dbg_d[:, 3, :], in_=wpre[:]).then_inc(
                    SEM["im"], 16)
                E.dma_start(out=dbg_d[:, 4, :], in_=rstd_a[:]).then_inc(
                    SEM["im"], 16)
                E.dma_start(out=dbg_d[:, 5, :], in_=stat2[:, :, 0],
                            allow_slow_non_contiguous=True).then_inc(
                    SEM["im"], 16)
            em("gpsimd", g_dbg)
            ctr["im"] += 96

        wait("sync", "st0", ST_TOT[0])
        wait("sync", "st1", ST_TOT[1])
        wait("sync", "sc0", ctr["sc0"])
        wait("sync", "sc1", ctr["sc1"])
        if cfg.get("DBG"):
            wait("sync", "im", ctr["im"])

        @block.sync
        def _(E):
            for f in ops["sync"]:
                f(E)

        @block.gpsimd
        def _(E):
            for f in ops["gpsimd"]:
                f(E)

        @block.tensor
        def _(E):
            for f in ops["tensor"]:
                f(E)

        @block.vector
        def _(E):
            for f in ops["vector"]:
                f(E)

        @block.scalar
        def _(E):
            for f in ops["scalar"]:
                f(E)

    return nc


def _pack_inputs(core_x, W1, b1, W2, b2, W3, b3, wr, W1f, b1f, W2f, b2f, cfg):
    D, F = cfg["D"], cfg["F"]
    FC = F // 128
    return {
        "x": np.ascontiguousarray(core_x),
        "w1": np.ascontiguousarray(W1.T),
        "b1": np.ascontiguousarray(b1.reshape(1, -1)),
        "w23": np.ascontiguousarray(np.concatenate([W2.T, W3.T], axis=1)),
        "b23": np.ascontiguousarray(np.concatenate([b2, b3]).reshape(1, -1)),
        "wr1": np.ascontiguousarray(wr.reshape(1, -1)),
        "w1f": np.ascontiguousarray(W1f),
        "b1fp": np.ascontiguousarray(b1f.reshape(FC, 128).T),
        "w2f": np.ascontiguousarray(W2f),
        "b2f": np.ascontiguousarray(b2f.reshape(1, -1)),
    }


def kernel(L0_attn_x, L0_attn_W1, L0_attn_b1, L0_attn_W2, L0_attn_b2,
           L0_attn_W3, L0_attn_b3, L0_attn_A, L0_ffn_w_router, L0_ffn_W1,
           L0_ffn_b1, L0_ffn_W2, L0_ffn_b2):
    cfg = CFG_FULL
    B, L, Dm = L0_attn_x.shape
    n_cores = 8
    rows = B // n_cores
    x = np.asarray(L0_attn_x, dtype=np.float32)

    nc = build_core_program(cfg)
    in_maps = []
    for c in range(n_cores):
        shard = x[c * rows:(c + 1) * rows].reshape(rows * L, Dm)
        in_maps.append(_pack_inputs(
            shard, np.asarray(L0_attn_W1), np.asarray(L0_attn_b1),
            np.asarray(L0_attn_W2), np.asarray(L0_attn_b2),
            np.asarray(L0_attn_W3), np.asarray(L0_attn_b3),
            np.asarray(L0_ffn_w_router), np.asarray(L0_ffn_W1),
            np.asarray(L0_ffn_b1), np.asarray(L0_ffn_W2),
            np.asarray(L0_ffn_b2), cfg))
    res = run_bass_kernel_spmd(nc, in_maps, core_ids=list(range(n_cores)))
    outs = [r["out"].reshape(rows, L, Dm) for r in res.results]
    return np.concatenate(outs, axis=0)


# version 53 (final)
# speedup vs baseline: 1.0726x; 1.0487x over previous
import numpy as np
from contextlib import ExitStack

import concourse.bass as bass
import concourse.mybir as mybir
from concourse.bass import IndirectOffsetOnAxis
from concourse.bass_utils import run_bass_kernel_spmd

F32 = mybir.dt.float32
BF16 = mybir.dt.bfloat16
I32 = mybir.dt.int32
ALU = mybir.AluOpType
AF = mybir.ActivationFunctionType

GELU_C2 = 0.044715
GELU_TANH_SCALE = 0.7978845608028654
LN_EPS = 1e-5
BIGV = 1.0e6

CFG_FULL = dict(D=1024, F=4096, SEQ=4096, ROWS=2, BI=23, LOHI=1.0)


def build_core_program(cfg):
    D, F, SEQ, ROWS = cfg["D"], cfg["F"], cfg["SEQ"], cfg["ROWS"]
    BI, LOHI = cfg["BI"], cfg["LOHI"]
    KC = D // 128
    FC = F // 128
    T = ROWS * SEQ
    NT = T // 128
    RC = SEQ // 128
    KTGT = SEQ // 2
    CT = ROWS * KTGT
    CNT = CT // 128
    GFC = 4
    NG = FC // GFC
    GW = GFC * 128
    halves = [(hh, nn, min(512, D - nn))
              for hh, nn in enumerate(range(0, D, 512))]

    nc = bass.Bass("TRN2", target_bir_lowering=False)

    x_d = nc.dram_tensor("x", [T, D], F32, kind="ExternalInput").ap()
    out_d = nc.dram_tensor("out", [T, D], F32, kind="ExternalOutput").ap()
    w1_d = nc.dram_tensor("w1", [D, D], F32, kind="ExternalInput").ap()
    b1_d = nc.dram_tensor("b1", [1, D], F32, kind="ExternalInput").ap()
    w23_d = nc.dram_tensor("w23", [D, 32], F32, kind="ExternalInput").ap()
    b23_d = nc.dram_tensor("b23", [1, 32], F32, kind="ExternalInput").ap()
    wr1_d = nc.dram_tensor("wr1", [1, D], F32, kind="ExternalInput").ap()
    w1f_d = nc.dram_tensor("w1f", [D, F], F32, kind="ExternalInput").ap()
    b1fp_d = nc.dram_tensor("b1fp", [128, FC], F32, kind="ExternalInput").ap()
    w2f_d = nc.dram_tensor("w2f", [F, D], F32, kind="ExternalInput").ap()
    b2f_d = nc.dram_tensor("b2f", [1, D], F32, kind="ExternalInput").ap()
    s_d = nc.dram_tensor("s_sc", [T, D], BF16, kind="Internal").ap()
    st3_d = nc.dram_tensor("st3", [T, 3], F32, kind="Internal").ap()
    imap_d = nc.dram_tensor("imap", [CT, 1], I32, kind="Internal").ap()
    if cfg.get("DBG"):
        dbg_d = nc.dram_tensor("dbg", [128, 6, NT], F32,
                               kind="ExternalOutput").ap()

    with ExitStack() as ctx:
        e = ctx.enter_context
        w1_b = e(nc.sbuf_tensor("w1_b", [128, KC, D], BF16))
        w23_b = e(nc.sbuf_tensor("w23_b", [128, KC, 32], BF16))
        w1f_b = e(nc.sbuf_tensor("w1f_b", [128, KC, F], BF16))
        w2f_b = e(nc.sbuf_tensor("w2f_b", [128, FC, D], BF16))
        b1_b = e(nc.sbuf_tensor("b1_b", [1, D], BF16))
        b23_b = e(nc.sbuf_tensor("b23_b", [1, 32], BF16))
        b1f_p = e(nc.sbuf_tensor("b1f_p", [128, FC], F32))
        b1fc = e(nc.sbuf_tensor("b1fc", [128, FC], F32))
        b2f_b = e(nc.sbuf_tensor("b2f_b", [1, D], BF16))

        ident_b = e(nc.sbuf_tensor("ident_b", [128, 128], BF16))
        ltinc_b = e(nc.sbuf_tensor("ltinc_b", [128, 128], BF16))
        ones_b = e(nc.sbuf_tensor("ones_b", [1, 128], BF16))
        twos_b = e(nc.sbuf_tensor("twos_b", [1, 128], BF16))
        onesf_r = e(nc.sbuf_tensor("onesf_r", [1, 128], F32))
        ones_cb = e(nc.sbuf_tensor("ones_cb", [128, 1], BF16))
        ones_cf = e(nc.sbuf_tensor("ones_cf", [128, 1], F32))
        CONSTS = [LN_EPS, -0.5, -1.0, 1.0, 0.5, GELU_TANH_SCALE,
                  GELU_C2, 2.0, 0.25, 0.0, GELU_C2 ** 0.5]
        cpool = e(nc.sbuf_tensor("cpool", [128, len(CONSTS)], F32))
        for ci, cv in enumerate(CONSTS):
            nc.const_aps.register(F32, cv, cpool[:, ci:ci + 1])
        stat2 = e(nc.sbuf_tensor("stat2", [128, NT, 2], F32))
        rstd_a = e(nc.sbuf_tensor("rstd_a", [128, NT], F32))
        scores = e(nc.sbuf_tensor("scores", [128, NT], F32))
        wpre = e(nc.sbuf_tensor("wpre", [128, NT], F32))
        st3s = e(nc.sbuf_tensor("st3s", [128, NT, 3], F32))
        tid_i = e(nc.sbuf_tensor("tid_i", [128, NT], I32))
        biga = e(nc.sbuf_tensor("biga", [128, NT], F32))
        rowb = e(nc.sbuf_tensor("rowb", [128, NT], F32))
        rank_i = e(nc.sbuf_tensor("rank_i", [128, NT], I32))
        imap_s = e(nc.sbuf_tensor("imap_s", [128, CNT], I32))
        statc3 = e(nc.sbuf_tensor("statc3", [128, 2, 3], F32))
        big2 = rank_i[:, 0:CNT]
        if NT >= 32:
            prod_t = wpre[:, 0:32].rearrange("p (a b) -> p a b", a=2)
        else:
            prod_t = e(nc.sbuf_tensor("prod_t", [128, 2, 16], F32))
        x_slab = e(nc.sbuf_tensor("x_slab", [128, 2, D], F32))
        xn_slab = e(nc.sbuf_tensor("xn_slab", [128, 2, D], F32))
        s_slab = e(nc.sbuf_tensor("s_slab", [128, 2, D], F32))
        xnb_c = e(nc.sbuf_tensor("xnb_c", [128, D], BF16))
        xnT_b = e(nc.sbuf_tensor("xnT_b", [128, 2, KC, 128], BF16))
        A_t = e(nc.sbuf_tensor("A_t", [128, 2, D], F32))
        B_t = e(nc.sbuf_tensor("B_t", [128, 2, D], F32))
        A0 = A_t[:, 0, :]
        B0 = B_t[:, 0, :]
        wr_row = A0[0:1, 0:D]
        V_t = e(nc.sbuf_tensor("V_t", [128, D], BF16))
        bn6 = e(nc.sbuf_tensor("bn6", [128, 12], F32))
        ln_t = e(nc.sbuf_tensor("ln_t", [128, 2], F32))
        bc_s = e(nc.sbuf_tensor("bc_s", [128, 2, 32], F32))
        cb_t = e(nc.sbuf_tensor("cb_t", [128, 2], F32))
        hT_g = [e(nc.sbuf_tensor(f"hT_g{j}", [128, GFC, 128], BF16))
                for j in range(3)]
        maskbf = xnb_c[:, 0:NT]
        mask_a = B0[:, 0:NT]
        rank_f = B0[:, NT:2 * NT]
        tmp_r = B0[:, 2 * NT:3 * NT]
        tmp_r2 = B0[:, 3 * NT:4 * NT]
        th_t = B0[:, 4 * NT:4 * NT + 2 * NT]
        lo_t = A0[0:1, 0:ROWS]
        hi_t = A0[0:1, ROWS:2 * ROWS]
        m_t = A0[0:1, 2 * ROWS:3 * ROWS]
        d_t = A0[0:1, 3 * ROWS:4 * ROWS]
        ge_t = A0[0:1, 4 * ROWS:5 * ROWS]
        mb_s = A0[:, 16:16 + ROWS]
        pcnt = A0[:, 24:24 + ROWS]
        tot_r = A0[0:1, 512:512 + NT]
        offsA = A0[0:1, 576:576 + NT]
        offsB = A0[0:1, 640:640 + NT]
        X_g = [A_t[:, 0, 0:GW], A_t[:, 1, 0:GW]]
        Y_g = [B_t[:, 0, 0:GW], B_t[:, 1, 0:GW]]
        V_g = [V_t[:, 0:GW], V_t[:, GW:2 * GW]]
        psA = e(nc.psum_tensor("psA", [128, D], F32))
        psB = e(nc.psum_tensor("psB", [128, D], F32))
        psC = e(nc.psum_tensor("psC", [128, D], F32))
        psD = e(nc.psum_tensor("psD", [128, D], BF16))
        psE = e(nc.psum_tensor("psE", [128, 512], F32))
        sem_names = ["x0", "x1", "ws", "wm", "wb", "pe", "dv", "ac", "gq",
                     "st0", "st1", "sd0", "sd1", "ga", "gb", "im", "sst",
                     "sc0", "sc1", "ix", "sx0", "sx1", "h0", "h1",
                     "gs0", "gs1"]
        SEM = {n: e(nc.semaphore("s_" + n)) for n in sem_names}
        block = e(nc.Block())

        ops = {k: [] for k in ("sync", "gpsimd", "tensor", "vector", "scalar")}
        ctr = {n: 0 for n in sem_names}
        ENG_SEM = {"vector": ("dv", SEM["dv"]), "scalar": ("ac", SEM["ac"]),
                   "tensor": ("pe", SEM["pe"]), "gpsimd": ("gq", SEM["gq"])}
        M = {}

        def em(eng, fn):
            ops[eng].append(fn)

        def inc(name, n=1):
            ctr[name] += n
            return ctr[name]

        def wait(eng, name, val):
            if val > 0:
                em(eng, lambda E, s=SEM[name], v=val: E.wait_ge(s, v))

        def waitd(eng, name, key):
            em(eng, lambda E, s=SEM[name], k=key: E.wait_ge(s, M[k]))

        def op(eng, fn, tag=None):
            nm, sem = ENG_SEM[eng]
            em(eng, lambda E, f=fn, S=sem: f(E).then_inc(S, 1))
            v = inc(nm)
            if tag is not None:
                M[tag] = v
            return v

        def opq(eng, fn):
            em(eng, fn)

        def dma(eng, sem_name, fn):
            em(eng, lambda E, f=fn, S=SEM[sem_name]: f(E).then_inc(S, 16))
            return inc(sem_name, 16)

        REGS = {}

        def g_setup(E):
            rt = E.alloc_register()
            E.reg_mov(rt, T - 1)
            REGS["T"] = rt
            rc = E.alloc_register()
            E.reg_mov(rc, CT - 1)
            REGS["CT"] = rc
            E.memset(ident_b[:], 1.0)
            E.drain()
            E.affine_select(out=ident_b[:], in_=ident_b[:], pattern=[[-1, 128]],
                            compare_op=ALU.is_equal, fill=0.0,
                            base=0, channel_multiplier=1)
            E.memset(ltinc_b[:], 1.0)
            E.drain()
            E.affine_select(out=ltinc_b[:], in_=ltinc_b[:], pattern=[[1, 128]],
                            compare_op=ALU.is_gt, fill=0.0,
                            base=1, channel_multiplier=-1)
            E.memset(ones_b[:], 1.0)
            E.memset(twos_b[:], 2.0)
            E.memset(onesf_r[:], 1.0)
            E.memset(ones_cb[:], 1.0)
            E.memset(ones_cf[:], 1.0)
            for ci, cv in enumerate(CONSTS):
                E.memset(cpool[:, ci:ci + 1], cv)
            E.iota(tid_i[:], pattern=[[128, NT]], base=0, channel_multiplier=1)
            E.iota(biga[:], pattern=[[128, NT]], base=int(BIGV),
                   channel_multiplier=1, allow_small_or_imprecise_dtypes=True)
            E.iota(rowb[:], pattern=[[KTGT, ROWS], [0, RC]], base=0,
                   channel_multiplier=0, allow_small_or_imprecise_dtypes=True)
            E.iota(big2[:], pattern=[[128, CNT]], base=2 * int(BIGV),
                   channel_multiplier=1)
            return E.drain()
        op("gpsimd", g_setup, "setup")
        wait("scalar", "gq", M["setup"])
        wait("vector", "gq", M["setup"])
        wait("tensor", "gq", M["setup"])

        dma("gpsimd", "im", lambda E: E.dma_start(
            out=imap_d.rearrange("(a b) o -> a (b o)", a=128), in_=big2[:]))
        IM_PREFILL = ctr["im"]

        small_loads = [
            (wr_row[:], wr1_d[:]),
            (w23_b[:], w23_d.rearrange("(c p) n -> p c n", p=128)),
            (b1_b[:], b1_d[:]),
            (b23_b[:], b23_d[:]),
            (b1f_p[:], b1fp_d[:]),
            (b2f_b[:], b2f_d[:]),
        ]
        for dst, src in small_loads:
            dma("gpsimd", "wm", lambda E, d=dst, s=src: E.dma_start(
                out=d, in_=s))
        W_SMALL = ctr["wm"]
        dma("gpsimd", "wb", lambda E: E.dma_start(
            out=w1_b[:], in_=w1_d.rearrange("(c p) n -> p c n", p=128)))
        dma("gpsimd", "wb", lambda E: E.dma_start(
            out=w1f_b[:], in_=w1f_d.rearrange("(c p) n -> p c n", p=128)))
        W_P12 = ctr["wb"]

        wait("vector", "wm", W_SMALL)
        op("vector", lambda E: E.tensor_scalar_mul(
            b1fc[:], b1f_p[:], GELU_C2 ** 0.5), "b1fc")

        wait("tensor", "wm", W_SMALL)
        opq("tensor", lambda E: E.matmul(psB[:, 0:512], onesf_r[0:1, :],
                                         wr_row[0:1, 0:512],
                                         start=True, stop=True))
        op("tensor", lambda E: E.matmul(psB[:, 512:1024], onesf_r[0:1, :],
                                        wr_row[0:1, 512:1024],
                                        start=True, stop=True), "wrbc")

        wait("tensor", "wb", W_P12)

        def p12_s1(i):
            par = i % 2
            xpar = "x%d" % par
            xc = x_slab[:, par, :]
            if i >= 2:
                wait("sync", "sx%d" % par, 16 * ((i - 2) // 2 + 1))
                wait("sync", "dv", M[("xn", i - 2)])
            dma("sync", "h%d" % par, lambda E, ii=i, b=xc: E.dma_start(
                out=b[:, 0:512], in_=x_d[128 * ii:128 * (ii + 1), 0:512]))
            xh = ctr["h%d" % par]
            dma("sync", xpar, lambda E, ii=i, b=xc: E.dma_start(
                out=b[:, 512:1024], in_=x_d[128 * ii:128 * (ii + 1),
                                            512:1024]))
            xv = ctr[xpar]
            wait("vector", "h%d" % par, xh)
            if i >= 1:
                wait("vector", "dv", M[("bn", i - 1)])
            opq("vector", lambda E, b=xc: E.bn_stats(
                bn6[:, 0:6], b[:, 0:512]))
            wait("vector", xpar, xv)
            op("vector", lambda E, b=xc: E.bn_stats(
                bn6[:, 6:12], b[:, 512:1024]), ("bns", i))
            wait("vector", "dv", M[("bns", i)])
            op("vector", lambda E, ii=i: E.bn_aggr(stat2[:, ii, :], bn6[:]),
               ("bn", i))

        def p12_s2(i):
            par = i % 2
            xc = x_slab[:, par, :]
            xnc = xn_slab[:, par, :]
            Bp = B_t[:, par, :]
            psP = psA if par == 0 else psC
            bce0 = 0 if par == 0 else 256

            wait("vector", "ac", M[("rstd", i)])
            if i >= 2:
                wait("vector", "ac", M[("Ssil", i - 2)])
                wait("vector", "dv", M[("s8", i - 2)])
            op("vector", lambda E, b=xc, o=xnc, ii=i: E.tensor_scalar(
                out=o, in0=b, scalar1=stat2[:, ii, 0:1],
                scalar2=rstd_a[:, ii:ii + 1], op0=ALU.subtract, op1=ALU.mult),
               ("xn", i))
            wait("sync", "h%d" % par, ctr["h%d" % par])
            wait("sync", "x%d" % par, ctr["x%d" % par])
            dma("sync", "sx%d" % par, lambda E, ii=i, b=xc: E.dma_start(
                out=out_d[128 * ii:128 * (ii + 1), :], in_=b))

            wait("vector", "dv", M[("xn", i)])
            if i >= 1:
                wait("vector", "pe", M[("btr", i - 1)])
            op("vector", lambda E, o=xnc: E.tensor_copy(xnb_c[:], o),
               ("cast", i))
            wait("tensor", "dv", M[("cast", i)])
            for k in range(KC):
                fn = (lambda E, kk=k: E.transpose(
                    psD[:, 128 * kk:128 * (kk + 1)],
                    xnb_c[:, 128 * kk:128 * (kk + 1)], ident_b[:]))
                if k == KC - 1:
                    op("tensor", fn, ("btr", i))
                else:
                    opq("tensor", lambda E, f=fn: f(E))
            if i == 0:
                wait("vector", "pe", M["wrbc"])
            op("vector", lambda E, o=xnc, ii=i: E.scalar_tensor_tensor(
                out=V_t[:], in0=o, scalar=1.0, in1=psB[:, 0:D],
                op0=ALU.mult, op1=ALU.mult,
                accum_out=scores[:, ii:ii + 1]), ("scr", i))
            wait("vector", "pe", M[("btr", i)])
            op("vector", lambda E, ii=i: E.tensor_copy(
                xnT_b[:, ii % 2, :, :],
                psD[:, 0:D].rearrange("p (k t) -> p k t", k=KC)),
               ("tev", i))

            wait("tensor", "dv", M[("tev", i)])
            if i >= 2:
                wait("tensor", "ac", M[("spl", i - 2)])
                wait("tensor", "dv", M[("s8", i - 2)])
            for h, n0, w in halves:
                opq("tensor", lambda E, nn=n0, ww=w, P=psP: E.matmul(
                    P[:, nn:nn + ww], ones_b[:], b1_b[0:1, nn:nn + ww],
                    start=True, stop=False))
                for k in range(KC):
                    fn = (lambda E, ii=i, kk=k, nn=n0, ww=w, P=psP:
                          E.matmul(P[:, nn:nn + ww], xnT_b[:, ii % 2, kk, :],
                                   w1_b[:, kk, nn:nn + ww],
                                   start=False, stop=(kk == KC - 1)))
                    if h == len(halves) - 1 and k == KC - 1:
                        op("tensor", fn, ("dmm", i))
                    else:
                        opq("tensor", lambda E, f=fn: f(E))
            if i >= 2:
                wait("tensor", "dv", M[("bce", i - 2)])
            opq("tensor", lambda E, b0=bce0: E.matmul(
                psE[:, b0:b0 + 32], ones_b[:], b23_b[:],
                start=True, stop=False))
            for k in range(KC):
                fn = (lambda E, ii=i, kk=k, b0=bce0: E.matmul(
                    psE[:, b0:b0 + 32], xnT_b[:, ii % 2, kk, :],
                    w23_b[:, kk, :], start=False, stop=(kk == KC - 1)))
                if k == KC - 1:
                    op("tensor", fn, ("bcm", i))
                else:
                    opq("tensor", lambda E, f=fn: f(E))

        def p12_s3(i):
            par = i % 2
            xc = x_slab[:, par, :]
            xnc = xn_slab[:, par, :]
            sc_ = s_slab[:, par, :]
            Ap = A_t[:, par, :]
            Bp = B_t[:, par, :]
            psP = psA if par == 0 else psC
            bce0 = 0 if par == 0 else 256

            wait("scalar", "pe", M[("dmm", i)])
            if i >= 2:
                wait("scalar", "dv", M[("q3", i - 2)])
            op("scalar", lambda E, P=psP, Bb=Bp: E.activation(
                out=Bb, in_=P[:, 0:D], func=AF.Exp), ("spe", i))
            wait("scalar", "ac", M[("spe", i)])
            if i >= 2:
                wait("scalar", "st%d" % par,
                     16 * ((i - 2) // 2 + 1))
            op("scalar", lambda E, Aa=Ap, Bb=Bp: E.activation(
                out=Aa, in_=Bb, func=AF.Ln, bias=1.0), ("spl", i))
            M[("dstt", i)] = M[("spl", i)]
            wait("scalar", "ac", M[("spl", i)])
            wait("scalar", "dv", M[("scr", i)])
            op("scalar", lambda E, o=xnc, Bb=Bp: E.activation(
                out=Bb, in_=o, func=AF.Silu), ("Ssil", i))
            wait("vector", "pe", M[("bcm", i)])
            op("vector", lambda E, ii=i, b0=bce0: E.tensor_copy(
                bc_s[:, ii % 2, :], psE[:, b0:b0 + 32]), ("bce", i))
            wait("vector", "dv", M[("bce", i)])
            op("vector", lambda E, ii=i: E.tensor_mul(
                prod_t[:, ii % 2, :], bc_s[:, ii % 2, 0:16],
                bc_s[:, ii % 2, 16:32]), None)
            wait("vector", "dv", ctr["dv"])
            op("vector", lambda E, ii=i: E.reduce_sum(
                cb_t[:, ii % 2:ii % 2 + 1], prod_t[:, ii % 2, :],
                axis=mybir.AxisListType.X), ("cb", i))
            wait("vector", "dv", M[("cb", i)])
            wait("vector", "ac", M[("spl", i)])
            if i >= 2:
                wait("vector", "sd%d" % par, 16 * ((i - 2) // 2 + 1))
                wait("vector", "dv", M[("q3", i - 2)])
            op("vector", lambda E, o=xnc, s2=sc_, ii=i, Aa=Ap:
               E.scalar_tensor_tensor(
                   out=s2, in0=o, scalar=cb_t[:, ii % 2:ii % 2 + 1], in1=Aa,
                   op0=ALU.mult, op1=ALU.mult), ("s8", i))
            wait("gpsimd", "dv", M[("s8", i)])
            dma("gpsimd", "sd%d" % par, lambda E, ii=i, s2=sc_: E.dma_start(
                out=s_d[128 * ii:128 * (ii + 1), :], in_=s2))

            wait("vector", "ac", M[("Ssil", i)])
            wait("vector", "dv", M[("s8", i)])
            op("vector", lambda E, s2=sc_, Aa=Ap, Bb=Bp: E.tensor_mul(
                Aa, s2, Bb), ("q3", i))
            wait("gpsimd", "sx%d" % par, 16 * (i // 2 + 1))
            wait("gpsimd", "dv", M[("q3", i)])
            dma("gpsimd", "st%d" % par, lambda E, ii=i, Aa=Ap: E.dma_start(
                out=out_d[128 * ii:128 * (ii + 1), :], in_=Aa,
                accum_op=ALU.add))
            i2 = i + 2
            if i2 < NT:
                wait("scalar", "dv", M[("bn", i2)])
                op("scalar", lambda E, ii=i2: E.activation(
                    out=ln_t[:, ii % 2:ii % 2 + 1], in_=stat2[:, ii, 1:2],
                    func=AF.Ln, bias=LN_EPS), ("lnv", i2))
                wait("scalar", "ac", M[("lnv", i2)])
                op("scalar", lambda E, ii=i2: E.activation(
                    out=rstd_a[:, ii:ii + 1],
                    in_=ln_t[:, ii % 2:ii % 2 + 1],
                    func=AF.Exp, scale=-0.5), ("rstd", i2))

        p12_s1(0)
        p12_s1(1)
        for i0 in (0, 1):
            wait("scalar", "dv", M[("bn", i0)])
            op("scalar", lambda E, ii=i0: E.activation(
                out=ln_t[:, ii % 2:ii % 2 + 1], in_=stat2[:, ii, 1:2],
                func=AF.Ln, bias=LN_EPS), ("lnv", i0))
            wait("scalar", "ac", M[("lnv", i0)])
            op("scalar", lambda E, ii=i0: E.activation(
                out=rstd_a[:, ii:ii + 1], in_=ln_t[:, ii % 2:ii % 2 + 1],
                func=AF.Exp, scale=-0.5), ("rstd", i0))
        p12_s2(0)
        for i in range(NT):
            if i + 2 < NT:
                p12_s1(i + 2)
            if i + 1 < NT:
                p12_s2(i + 1)
            p12_s3(i)

        ST_TOT = {0: ctr["st0"], 1: ctr["st1"]}
        SD_TOT = (ctr["sd0"], ctr["sd1"])

        dma("gpsimd", "wb", lambda E: E.dma_start(
            out=w2f_b[:], in_=w2f_d.rearrange("(c p) n -> p c n", p=128)))
        W_ALL = ctr["wb"]

        wait("scalar", "dv", M[("scr", NT - 1)])
        wait("scalar", "dv", M[("cb", NT - 1)])
        op("scalar", lambda E: E.activation(
            out=wpre[:], in_=scores[:], func=AF.Exp, scale=-1.0), "wpe")
        wait("scalar", "ac", M["wpe"])
        op("scalar", lambda E: E.activation(
            out=wpre[:], in_=wpre[:], func=AF.Identity, bias=1.0), "wp1")
        wait("vector", "ac", M["wp1"])
        op("vector", lambda E: E.reciprocal(wpre[:], wpre[:]), "wpr")
        wait("vector", "dv", M["wpr"])
        op("vector", lambda E: E.tensor_scalar_mul(wpre[:], wpre[:], 0.5),
           "wps")

        wait("vector", "dv", M["wps"])
        opq("vector", lambda E: E.tensor_copy(st3s[:, :, 0],
                                              stat2[:, :, 0]))
        opq("vector", lambda E: E.tensor_copy(st3s[:, :, 1], rstd_a[:]))
        op("vector", lambda E: E.tensor_copy(st3s[:, :, 2], wpre[:]),
           "st3c")
        wait("sync", "dv", M["st3c"])
        NT2 = NT // 2
        for hh in range(2):
            dma("sync", "sst", lambda E, h2=hh: E.dma_start(
                out=st3_d[128 * NT2 * h2:128 * NT2 * (h2 + 1), :].rearrange(
                    "(i p) k -> p i k", p=128),
                in_=st3s[:, NT2 * h2:NT2 * (h2 + 1), :],
                allow_slow_non_contiguous=True))
        SST_TOT = ctr["sst"]

        wait("vector", "dv", ctr["dv"])
        wait("vector", "ac", M[("Ssil", NT - 1)])
        wait("vector", "st0", ST_TOT[0])
        wait("vector", "st1", ST_TOT[1])
        op("vector", lambda E: E.memset(lo_t, -LOHI), None)
        op("vector", lambda E: E.memset(hi_t, LOHI), "binit")

        for j in range(BI):
            wait("vector", "dv", ctr["dv"])
            op("vector", lambda E: E.tensor_add(m_t, lo_t, hi_t), None)
            wait("vector", "dv", ctr["dv"])
            op("vector", lambda E: E.tensor_scalar_mul(m_t, m_t, 0.5),
               ("bm", j))
            wait("tensor", "dv", M[("bm", j)])
            op("tensor", lambda E: E.matmul(psE[:, 0:ROWS], onesf_r[:], m_t,
                                            start=True, stop=True), ("bmb", j))
            wait("vector", "pe", M[("bmb", j)])
            for r in range(ROWS):
                fn = (lambda E, rr=r: E.tensor_scalar(
                    out=th_t[:, rr * RC:(rr + 1) * RC],
                    in0=scores[:, rr * RC:(rr + 1) * RC],
                    scalar1=psE[:, rr:rr + 1], scalar2=0.0,
                    op0=ALU.is_gt, op1=ALU.add,
                    accum_out=pcnt[:, rr:rr + 1]))
                if r == ROWS - 1:
                    op("vector", fn, ("bcmp", j))
                else:
                    opq("vector", lambda E, f=fn: f(E))
            wait("tensor", "dv", M[("bcmp", j)])
            op("tensor", lambda E: E.matmul(psE[0:1, 4:4 + ROWS], ones_cf[:],
                                            pcnt, start=True, stop=True),
               ("bcnt", j))
            wait("vector", "pe", M[("bcnt", j)])
            op("vector", lambda E: E.tensor_scalar(
                out=ge_t, in0=psE[0:1, 4:4 + ROWS], scalar1=float(KTGT),
                scalar2=None, op0=ALU.is_ge), None)
            op("vector", lambda E: E.tensor_sub(d_t, m_t, lo_t), None)
            wait("vector", "dv", ctr["dv"])
            op("vector", lambda E: E.tensor_mul(d_t, d_t, ge_t), None)
            wait("vector", "dv", ctr["dv"])
            op("vector", lambda E: E.tensor_add(lo_t, lo_t, d_t), None)
            wait("vector", "dv", ctr["dv"])
            op("vector", lambda E: E.tensor_sub(d_t, hi_t, m_t), None)
            wait("vector", "dv", ctr["dv"])
            op("vector", lambda E: E.tensor_mul(d_t, d_t, ge_t), None)
            wait("vector", "dv", ctr["dv"])
            op("vector", lambda E: E.tensor_add(hi_t, m_t, d_t), ("bupd", j))

        wait("tensor", "dv", M[("bupd", BI - 1)])
        op("tensor", lambda E: E.matmul(psE[:, 0:ROWS], onesf_r[:], lo_t,
                                        start=True, stop=True), "bmaskmb")
        wait("vector", "pe", M["bmaskmb"])
        op("vector", lambda E: E.tensor_copy(mb_s, psE[:, 0:ROWS]), "bmbc2")
        wait("vector", "dv", M["bmbc2"])
        for r in range(ROWS):
            fn = (lambda E, rr=r: E.tensor_scalar(
                out=mask_a[:, rr * RC:(rr + 1) * RC],
                in0=scores[:, rr * RC:(rr + 1) * RC],
                scalar1=mb_s[:, rr:rr + 1], scalar2=None, op0=ALU.is_gt))
            if r == ROWS - 1:
                op("vector", fn, "bmask")
            else:
                opq("vector", lambda E, f=fn: f(E))

        wait("vector", "dv", M["bmask"])
        wait("vector", "pe", M[("btr", NT - 1)])
        op("vector", lambda E: E.tensor_copy(maskbf, mask_a), "mkb")
        wait("tensor", "dv", M["mkb"])
        op("tensor", lambda E: E.matmul(psE[:, 256:256 + NT], ltinc_b[:],
                                        maskbf, start=True, stop=True),
           "incl")
        op("tensor", lambda E: E.matmul(psE[0:1, 320:320 + NT], ones_cb[:],
                                        maskbf, start=True, stop=True), "totm")
        wait("vector", "pe", M["totm"])
        op("vector", lambda E: E.tensor_copy(rank_f, psE[:, 256:256 + NT]),
           "rk0")
        op("vector", lambda E: E.tensor_copy(tot_r, psE[0:1, 320:320 + NT]),
           "totc")
        wait("vector", "dv", M["totc"])
        for r in range(ROWS):
            a = offsA[0:1, r * RC:(r + 1) * RC]
            b = offsB[0:1, r * RC:(r + 1) * RC]
            t = tot_r[0:1, r * RC:(r + 1) * RC]
            opq("vector", lambda E, aa=a: E.memset(aa[0:1, 0:1], 0.0))
            op("vector", lambda E, aa=a, tt=t: E.tensor_copy(
                aa[0:1, 1:RC], tt[0:1, 0:RC - 1]), ("cse", r, 0))
            src, dst = a, b
            kk_ = 1
            step = 0
            while kk_ < RC:
                wait("vector", "dv", M[("cse", r, step)])
                opq("vector", lambda E, s3=src, d3=dst, k4=kk_: E.tensor_copy(
                    d3[0:1, 0:k4], s3[0:1, 0:k4]))
                op("vector", lambda E, s3=src, d3=dst, k4=kk_: E.tensor_add(
                    d3[0:1, k4:RC], s3[0:1, k4:RC], s3[0:1, 0:RC - k4]),
                   ("cse", r, step + 1))
                src, dst = dst, src
                kk_ *= 2
                step += 1
            if src is not offsA:
                wait("vector", "dv", M[("cse", r, step)])
                op("vector", lambda E, s3=src, aa=a: E.tensor_copy(aa, s3),
                   ("cse", r, step + 1))
            M[("csef", r)] = ctr["dv"]
        wait("tensor", "dv", max(M[("csef", r)] for r in range(ROWS)))
        op("tensor", lambda E: E.matmul(psE[:, 384:384 + NT], onesf_r[:],
                                        offsA[0:1, 0:NT], start=True,
                                        stop=True), "offb")
        wait("vector", "pe", M["offb"])
        op("vector", lambda E: E.tensor_sub(rank_f, rank_f, mask_a), None)
        wait("vector", "dv", ctr["dv"])
        op("vector", lambda E: E.tensor_add(rank_f, rank_f,
                                            psE[:, 384:384 + NT]), "rk1")
        wait("vector", "dv", M["rk1"])
        op("vector", lambda E: E.tensor_scalar(
            out=tmp_r, in0=rank_f, scalar1=float(KTGT), scalar2=None,
            op0=ALU.is_ge), None)
        op("vector", lambda E: E.tensor_scalar(
            out=tmp_r2, in0=mask_a, scalar1=-1.0, scalar2=1.0,
            op0=ALU.mult, op1=ALU.add), None)
        wait("vector", "dv", ctr["dv"])
        op("vector", lambda E: E.tensor_add(tmp_r, tmp_r, tmp_r2), None)
        wait("vector", "dv", ctr["dv"])
        op("vector", lambda E: E.tensor_mul(tmp_r, tmp_r, biga[:]), None)
        wait("vector", "dv", ctr["dv"])
        op("vector", lambda E: E.tensor_add(rank_f, rank_f, tmp_r), None)
        wait("vector", "dv", ctr["dv"])
        op("vector", lambda E: E.tensor_add(rank_f, rank_f, rowb[:]), "rk2")
        wait("vector", "dv", M["rk2"])
        wait("vector", "im", IM_PREFILL)
        op("vector", lambda E: E.tensor_copy(rank_i[:], rank_f), "rki")

        def emit_im_scatter(c):
            dma("gpsimd", "im", lambda E, cc=c: E.indirect_dma_start(
                out=imap_d, out_offset=IndirectOffsetOnAxis(
                    ap=rank_i[:, cc:cc + 1], axis=0),
                in_=tid_i[:, cc:cc + 1], in_offset=None,
                bounds_check=REGS["CT"], oob_is_err=False))
        wait("gpsimd", "dv", M["rki"])
        wait("gpsimd", "im", IM_PREFILL)
        for c in range(RC):
            emit_im_scatter(c)
        IM_H1 = ctr["im"]
        wait("sync", "im", IM_H1)
        dma("sync", "ix", lambda E: E.dma_start(
            out=imap_s[:, 0:CNT // 2],
            in_=imap_d[0:CT // 2, :].rearrange("(c p) o -> p (c o)", p=128),
            allow_slow_non_contiguous=True))
        IMS_V = ctr["ix"]

        wait("tensor", "wb", W_ALL)
        wait("gpsimd", "ix", IMS_V)
        wait("gpsimd", "sd0", SD_TOT[0])
        wait("gpsimd", "sd1", SD_TOT[1])
        wait("gpsimd", "sst", SST_TOT)

        def emit_gathers(j):
            c = j % 2
            gpar = "ga" if c == 0 else "gb"
            if j >= CNT // 2:
                wait("gpsimd", "ix", 32)
            if j >= 2:
                wait("gpsimd", "dv", M[("Vo", j - 2)])
                wait("gpsimd", "sc%d" % (j % 2),
                     16 * ((j - 2) // 2 + 1))
            dma("gpsimd", gpar, lambda E, jj=j, cc=c: E.indirect_dma_start(
                out=statc3[:, cc, :], out_offset=None, in_=st3_d,
                in_offset=IndirectOffsetOnAxis(
                    ap=imap_s[:, jj:jj + 1], axis=0),
                bounds_check=REGS["T"], oob_is_err=False))
            dma("gpsimd", gpar, lambda E, jj=j, cc=c: E.indirect_dma_start(
                out=x_slab[:, cc, :], out_offset=None, in_=x_d,
                in_offset=IndirectOffsetOnAxis(
                    ap=imap_s[:, jj:jj + 1], axis=0),
                bounds_check=REGS["T"], oob_is_err=False))
            M[("gat", j)] = ctr[gpar]
            dma("gpsimd", "gs%d" % c, lambda E, jj=j, cc=c:
                E.indirect_dma_start(
                    out=s_slab[:, cc, :], out_offset=None, in_=s_d,
                    in_offset=IndirectOffsetOnAxis(
                        ap=imap_s[:, jj:jj + 1], axis=0),
                    bounds_check=REGS["T"], oob_is_err=False))
            M[("gats", j)] = ctr["gs%d" % c]

        PB2 = [psB[:, 0:512], psB[:, 512:1024], psE[:, 0:512]]
        VFE = {}

        def emit_head_v(j):
            c = j % 2
            gpar = "ga" if c == 0 else "gb"
            xc = x_slab[:, c, :]
            xnc = xn_slab[:, c, :]

            wait("vector", gpar, M[("gat", j)])
            if j < 2:
                wait("vector", "st%d" % (j % 2), ST_TOT[j % 2])
            else:
                wait("vector", "dv", M[("Vo", j - 2)])
            op("vector", lambda E, b=xc, o=xnc, cc=c: E.tensor_scalar(
                out=o, in0=b, scalar1=statc3[:, cc, 0:1],
                scalar2=statc3[:, cc, 1:2], op0=ALU.subtract, op1=ALU.mult),
               ("xnb2", j))
            wait("vector", "dv", M[("xnb2", j)])
            if j >= 1:
                wait("vector", "pe", M[("btr2", j - 1)])
            op("vector", lambda E, o=xnc: E.tensor_copy(xnb_c[:], o),
               ("cst2", j))

        def emit_head_pe(j):
            wait("tensor", "dv", M[("cst2", j)])
            for k in range(KC):
                fn = (lambda E, kk=k: E.transpose(
                    psD[:, 128 * kk:128 * (kk + 1)],
                    xnb_c[:, 128 * kk:128 * (kk + 1)], ident_b[:]))
                if k == KC - 1:
                    op("tensor", fn, ("btr2", j))
                else:
                    opq("tensor", lambda E, f=fn: f(E))
            wait("vector", "pe", M[("btr2", j)])
            op("vector", lambda E, jj=j: E.tensor_copy(
                xnT_b[:, jj % 2, :, :],
                psD[:, 0:D].rearrange("p (k t) -> p k t", k=KC)),
               ("tev2", j))

        def emit_body(j):
            c = j % 2
            xc = x_slab[:, c, :]
            xnc = xn_slab[:, c, :]
            sc_ = s_slab[:, c, :]
            psM = psC if c == 0 else psA

            def emit_m1(g):
                G = NG * j + g
                pb = PB2[G % 3].rearrange("p (j t) -> p j t", j=GFC)
                if g == 0:
                    wait("tensor", "dv", M[("tev2", j)])
                if G >= 3:
                    pg = divmod(G - 3, NG)
                    wait("tensor", VFE[pg], M[("vfree",) + pg])
                for jj in range(GFC):
                    fc = GFC * g + jj
                    for k in range(KC):
                        fn = (lambda E, f=fc, j2=jj, kk=k, P=pb, jc=j:
                              E.matmul(P[:, j2, :],
                                       w1f_b[:, kk, 128 * f:128 * (f + 1)],
                                       xnT_b[:, jc % 2, kk, :],
                                       start=(kk == 0), stop=(kk == KC - 1)))
                        if jj == GFC - 1 and k == KC - 1:
                            op("tensor", fn, ("m1", j, g))
                        else:
                            opq("tensor", lambda E, f=fn: f(E))

            def emit_evac(g):
                G = NG * j + g
                b = G % 2
                bk = G % 3
                eng = "vector"
                VFE[(j, g)] = "dv"
                wait(eng, "pe", M[("m1", j, g)])
                if G >= 2:
                    pg = divmod(G - 2, NG)
                    wait(eng, "ac", M[("sq",) + pg])
                    wait(eng, "dv", M[("h",) + pg])
                for jj in range(GFC):
                    fc = GFC * g + jj
                    if eng == "vector":
                        fn = (lambda E, bb=b, bkk=bk, j2=jj, f=fc:
                              E.tensor_scalar(
                                  out=V_t[:, bb * GW + j2 * 128:
                                          bb * GW + (j2 + 1) * 128],
                                  in0=PB2[bkk][:, j2 * 128:(j2 + 1) * 128],
                                  scalar1=b1f_p[:, f:f + 1], scalar2=None,
                                  op0=ALU.add))
                    else:
                        fn = (lambda E, bb=b, bkk=bk, j2=jj, f=fc:
                              E.activation(
                                  out=V_t[:, bb * GW + j2 * 128:
                                          bb * GW + (j2 + 1) * 128],
                                  in_=PB2[bkk][:, j2 * 128:(j2 + 1) * 128],
                                  func=AF.Identity, bias=b1f_p[:, f:f + 1]))
                    if jj == GFC - 1:
                        op(eng, fn, ("vfree", j, g))
                    else:
                        opq(eng, lambda E, f=fn: f(E))

            def emit_sq(g):
                G = NG * j + g
                b = G % 2
                wait("scalar", VFE[(j, g)], M[("vfree", j, g)])
                if G >= 2:
                    pg = divmod(G - 2, NG)
                    wait("scalar", "dv", M[("h",) + pg])
                op("scalar", lambda E, bb=b: E.activation(
                    out=X_g[bb], in_=V_g[bb], func=AF.Square,
                    scale=GELU_C2 ** 0.5), ("sq", j, g))

            def emit_u(gj, g):
                G = NG * gj + g
                b = G % 2
                wait("vector", "ac", M[("sq", gj, g)])
                if G >= 2:
                    pg = divmod(G - 2, NG)
                    wait("vector", "ac", M[("tanh",) + pg])
                if G >= 1 and g == 0:
                    wait("vector", "dv", M[("Vo", gj - 1)])
                op("vector", lambda E, bb=b: E.scalar_tensor_tensor(
                    out=Y_g[bb], in0=X_g[bb], scalar=1.0, in1=V_g[bb],
                    op0=ALU.add, op1=ALU.mult), ("u", gj, g))

            def emit_tanh(gj, g):
                wait("scalar", "dv", M[("u", gj, g)])
                op("scalar", lambda E, bb=(NG * gj + g) % 2: E.activation(
                    out=X_g[bb], in_=Y_g[bb], func=AF.Tanh,
                    scale=GELU_TANH_SCALE), ("tanh", gj, g))

            def emit_h(gj, g):
                G = NG * gj + g
                b = G % 2
                wait("vector", "ac", M[("tanh", gj, g)])
                if G >= 3:
                    pg = divmod(G - 3, NG)
                    wait("vector", "pe", M[("m2p",) + pg])
                op("vector", lambda E, bb=b, b3=G % 3: E.scalar_tensor_tensor(
                    out=hT_g[b3].rearrange("p j t -> p (j t)"),
                    in0=X_g[bb], scalar=1.0, in1=V_g[bb],
                    op0=ALU.add, op1=ALU.mult), ("h", gj, g))

            def emit_pbias():
                wait("tensor", "dv", M[("tev2", j)])
                if j >= 2:
                    wait("tensor", "dv", M[("Vy", j - 2)])
                for h, n0, w in halves:
                    fn = (lambda E, nn=n0, ww=w, P=psM: E.matmul(
                        P[:, nn:nn + ww], twos_b[:], b2f_b[0:1, nn:nn + ww],
                        start=True, stop=False))
                    if h == len(halves) - 1:
                        op("tensor", fn, ("pbias", j))
                    else:
                        opq("tensor", lambda E, f=fn: f(E))

            def emit_m2p(gj, g):
                G = NG * gj + g
                b = G % 3
                wait("tensor", "dv", M[("h", gj, g)])
                if gj == j and g == 0:
                    wait("tensor", "pe", M[("pbias", j)])
                n_mm = len(halves) * GFC
                mi = 0
                for h, n0, w in halves:
                    for jj in range(GFC):
                        fc = GFC * g + jj
                        mi += 1
                        stop = (g == NG - 1 and jj == GFC - 1)
                        fn = (lambda E, f=fc, j2=jj, nn=n0, ww=w,
                              P=(psC if gj % 2 == 0 else psA),
                              bb=b, st=stop: E.matmul(
                                  P[:, nn:nn + ww], hT_g[bb][:, j2, :],
                                  w2f_b[:, f, nn:nn + ww],
                                  start=False, stop=st))
                        if mi == n_mm:
                            op("tensor", fn, ("m2p", gj, g))
                        else:
                            opq("tensor", lambda E, f=fn: f(E))

            def back(k, d):
                gg = k - d
                if gg >= NG:
                    return None
                if gg >= 0:
                    return (j, gg)
                if j == 0:
                    return None
                return (j - 1, NG + gg)

            for k in range(NG + 4):
                if k < NG:
                    emit_m1(k)
                if k == 5 and j + 1 < CNT:
                    emit_head_v(j + 1)
                if k == 6 and j + 1 < CNT:
                    emit_head_pe(j + 1)
                t1 = back(k, 1)
                if k == 1 and j + 1 < CNT:
                    emit_gathers(j + 1)
                if t1 and t1[0] == j:
                    emit_u(*t1)
                    emit_tanh(*t1)
                t2 = back(k, 2)
                if t2 and t2[0] == j:
                    emit_h(*t2)
                if k == 2:
                    emit_pbias()
                t4 = back(k, 4)
                if t4 and t4[0] == j:
                    emit_m2p(*t4)
                if k < NG:
                    emit_evac(k)
                    emit_sq(k)
            M[("m2d", j)] = M[("m2p", j, NG - 1)]

            wait("vector", "pe", M[("m2d", j)])
            op("vector", lambda E, o=xnc, P=psM, cc=c: E.scalar_tensor_tensor(
                out=o, in0=P[:, 0:D], scalar=statc3[:, cc, 2:3], in1=o,
                op0=ALU.mult, op1=ALU.add), ("Vy", j))
            wait("scalar", "dv", M[("Vy", j)])
            op("scalar", lambda E, o=xnc: E.activation(
                out=B0, in_=o, func=AF.Silu), ("Sy", j))
            wait("vector", "ac", M[("Sy", j)])
            wait("vector", "gs%d" % c, M[("gats", j)])
            op("vector", lambda E, o=xnc, s2=sc_: E.tensor_mul(
                o, B0, s2), ("Vt", j))
            wait("vector", "dv", M[("Vt", j)])
            op("vector", lambda E, o=xnc, b=xc: E.tensor_add(b, b, o),
               ("Vo", j))
            if j == 0:
                wait("gpsimd", "st0", ST_TOT[0])
                wait("gpsimd", "st1", ST_TOT[1])
            wait("gpsimd", "dv", M[("Vo", j)])
            dma("gpsimd", "sc%d" % c, lambda E, jj=j, b=xc:
                E.indirect_dma_start(
                    out=out_d, out_offset=IndirectOffsetOnAxis(
                        ap=imap_s[:, jj:jj + 1], axis=0),
                    in_=b, in_offset=None,
                    bounds_check=REGS["T"], oob_is_err=False))

        emit_gathers(0)
        emit_head_v(0)
        emit_head_pe(0)
        rest = list(range(RC, NT))
        nb = max(1, CNT // 2 - 2)
        per = max(1, (len(rest) + nb - 1) // nb)
        for j in range(CNT):
            emit_body(j)
            if rest:
                for c in rest[:per]:
                    emit_im_scatter(c)
                rest = rest[per:]
                if not rest:
                    wait("sync", "im", ctr["im"])
                    dma("sync", "ix", lambda E: E.dma_start(
                        out=imap_s[:, CNT // 2:CNT],
                        in_=imap_d[CT // 2:CT, :].rearrange(
                            "(c p) o -> p (c o)", p=128),
                        allow_slow_non_contiguous=True))

        if cfg.get("DBG"):
            wait("gpsimd", "dv", M["rki"])
            def g_dbg(E):
                E.dma_start(out=dbg_d[:, 0, :], in_=scores[:]).then_inc(
                    SEM["im"], 16)
                E.dma_start(out=dbg_d[:, 1, :], in_=mask_a).then_inc(
                    SEM["im"], 16)
                E.dma_start(out=dbg_d[:, 2, :], in_=rank_f).then_inc(
                    SEM["im"], 16)
                E.dma_start(out=dbg_d[:, 3, :], in_=wpre[:]).then_inc(
                    SEM["im"], 16)
                E.dma_start(out=dbg_d[:, 4, :], in_=rstd_a[:]).then_inc(
                    SEM["im"], 16)
                E.dma_start(out=dbg_d[:, 5, :], in_=stat2[:, :, 0],
                            allow_slow_non_contiguous=True).then_inc(
                    SEM["im"], 16)
            em("gpsimd", g_dbg)
            ctr["im"] += 96

        wait("sync", "st0", ST_TOT[0])
        wait("sync", "st1", ST_TOT[1])
        wait("sync", "sc0", ctr["sc0"])
        wait("sync", "sc1", ctr["sc1"])
        if cfg.get("DBG"):
            wait("sync", "im", ctr["im"])

        @block.sync
        def _(E):
            for f in ops["sync"]:
                f(E)

        @block.gpsimd
        def _(E):
            for f in ops["gpsimd"]:
                f(E)

        @block.tensor
        def _(E):
            for f in ops["tensor"]:
                f(E)

        @block.vector
        def _(E):
            for f in ops["vector"]:
                f(E)

        @block.scalar
        def _(E):
            for f in ops["scalar"]:
                f(E)

    return nc


def _pack_inputs(core_x, W1, b1, W2, b2, W3, b3, wr, W1f, b1f, W2f, b2f, cfg):
    D, F = cfg["D"], cfg["F"]
    FC = F // 128
    return {
        "x": np.ascontiguousarray(core_x),
        "w1": np.ascontiguousarray(W1.T),
        "b1": np.ascontiguousarray(b1.reshape(1, -1)),
        "w23": np.ascontiguousarray(np.concatenate([W2.T, W3.T], axis=1)),
        "b23": np.ascontiguousarray(np.concatenate([b2, b3]).reshape(1, -1)),
        "wr1": np.ascontiguousarray(wr.reshape(1, -1)),
        "w1f": np.ascontiguousarray(W1f),
        "b1fp": np.ascontiguousarray(b1f.reshape(FC, 128).T),
        "w2f": np.ascontiguousarray(W2f),
        "b2f": np.ascontiguousarray(b2f.reshape(1, -1)),
    }


def kernel(L0_attn_x, L0_attn_W1, L0_attn_b1, L0_attn_W2, L0_attn_b2,
           L0_attn_W3, L0_attn_b3, L0_attn_A, L0_ffn_w_router, L0_ffn_W1,
           L0_ffn_b1, L0_ffn_W2, L0_ffn_b2):
    cfg = CFG_FULL
    B, L, Dm = L0_attn_x.shape
    n_cores = 8
    rows = B // n_cores
    x = np.asarray(L0_attn_x, dtype=np.float32)

    nc = build_core_program(cfg)
    in_maps = []
    for c in range(n_cores):
        shard = x[c * rows:(c + 1) * rows].reshape(rows * L, Dm)
        in_maps.append(_pack_inputs(
            shard, np.asarray(L0_attn_W1), np.asarray(L0_attn_b1),
            np.asarray(L0_attn_W2), np.asarray(L0_attn_b2),
            np.asarray(L0_attn_W3), np.asarray(L0_attn_b3),
            np.asarray(L0_ffn_w_router), np.asarray(L0_ffn_W1),
            np.asarray(L0_ffn_b1), np.asarray(L0_ffn_W2),
            np.asarray(L0_ffn_b2), cfg))
    res = run_bass_kernel_spmd(nc, in_maps, core_ids=list(range(n_cores)))
    outs = [r["out"].reshape(rows, L, Dm) for r in res.results]
    return np.concatenate(outs, axis=0)
